# version 05
# baseline (speedup 1.0000x reference)
import contextlib
import numpy as np

import concourse.bass as bass
import concourse.mybir as mybir
from concourse.alu_op_type import AluOpType as OP
from concourse.bass_utils import run_bass_kernel_spmd

F32 = mybir.dt.float32
BF16 = mybir.dt.bfloat16
I32 = mybir.dt.int32
AF = mybir.ActivationFunctionType

DM = 1024
DI = 2048
DI2 = 4096
NST = 16
DTR = 64
HID = 3072
NCORE = 8
CC = DM // 128
DC = DI // 128
OC = DI2 // 128
HC = HID // 128
S127 = float(np.float32(1.0 / 127.0))
S127SQ = float(np.float32(1.0 / 127.0) * np.float32(1.0 / 127.0))


def _prep_weights(inp):
    c = np.ascontiguousarray
    w = {}
    qkv = inp["L0_attn_a_qkv_w"]
    w["wqT"] = c(qkv[0:DM].T)
    w["wkT"] = c(qkv[DM:2 * DM].T)
    w["wvT"] = c(qkv[2 * DM:3 * DM].T)
    qb = inp["L0_attn_a_qkv_b"]
    w["bq"] = c(qb[None, 0:DM])
    w["bk"] = c(qb[None, DM:2 * DM])
    w["bv"] = c(qb[None, 2 * DM:3 * DM])
    w["wpT"] = c(inp["L0_attn_a_proj_w"].T)
    w["bp"] = c(inp["L0_attn_a_proj_b"][None, :])
    w["normw"] = c(inp["L0_attn_a_norm_w"][None, :])
    w["normb"] = c(inp["L0_attn_a_norm_b"][None, :])
    w["wgT"] = c(inp["L0_ffn_w_gate"].T)
    w["wuT"] = c(inp["L0_ffn_w_up"].T)
    w["wdT"] = c(inp["L0_ffn_w_down"].T)
    w["winT"] = c(inp["L0_attn_b_W_in"].T)
    w["woutT"] = c(inp["L0_attn_b_W_out"].T)
    w["wxr"] = c(inp["L0_attn_b_W_xproj"].T.reshape(DC, 128, 96).transpose(1, 0, 2).reshape(128, DC * 96))
    w["wdtT"] = c(inp["L0_attn_b_W_dt"].T)
    r = lambda v: c(v.reshape(DC, 128).T)
    w["bconv"] = r(inp["L0_attn_b_b_conv"])
    w["bdt"] = r(inp["L0_attn_b_b_dt"])
    w["dvec"] = r(inp["L0_attn_b_D"])
    w["wconv"] = c(inp["L0_attn_b_W_conv"].reshape(DI, 4).reshape(DC, 128, 4).transpose(1, 0, 2).reshape(128, DC * 4))
    w["alog"] = c(inp["L0_attn_b_A_log"].reshape(DC, 128, NST).transpose(1, 0, 2).reshape(128, DC * NST))
    w["iden"] = np.eye(128, dtype=np.float32)
    sel = np.zeros((NST, NST * 128), np.float32)
    for n in range(NST):
        sel[n, 128 * n:128 * (n + 1)] = 1.0
    w["sel"] = sel
    return w


def build(T=2048, debug=False):
    TT = T // 128
    NSC = T // 256
    nc = bass.Bass("TRN2", target_bir_lowering=False)

    dt_in = {
        "x": ([T, DM], F32), "wqT": ([DM, DM], F32), "wkT": ([DM, DM], F32),
        "wvT": ([DM, DM], F32), "bq": ([1, DM], F32), "bk": ([1, DM], F32),
        "bv": ([1, DM], F32), "wpT": ([DM, DM], F32), "bp": ([1, DM], F32),
        "normw": ([1, DM], F32), "normb": ([1, DM], F32),
        "wgT": ([DM, HID], I32), "wuT": ([DM, HID], I32), "wdT": ([HID, DM], I32),
        "winT": ([DM, DI2], F32), "woutT": ([DI, DM], F32),
        "wxr": ([128, DC * 96], F32), "wdtT": ([DTR, DI], F32),
        "bconv": ([128, DC], F32), "bdt": ([128, DC], F32), "dvec": ([128, DC], F32),
        "wconv": ([128, DC * 4], F32), "alog": ([128, DC * NST], F32),
        "iden": ([128, 128], F32), "sel": ([NST, NST * 128], F32),
    }
    A = {k: nc.dram_tensor(k, sh, dt, kind="ExternalInput").ap() for k, (sh, dt) in dt_in.items()}
    out = nc.dram_tensor("out", [T, DM], F32, kind="ExternalOutput").ap()
    dbg = {}
    if debug:
        for k in ("dbg_attn", "dbg_ffn1", "dbg_mamba"):
            dbg[k] = nc.dram_tensor(k, [T, DM], F32, kind="ExternalOutput").ap()

    st_wg = nc.dram_tensor("st_wg", [DM, HID], BF16, kind="Internal").ap()
    st_wu = nc.dram_tensor("st_wu", [DM, HID], BF16, kind="Internal").ap()
    st_wd = nc.dram_tensor("st_wd", [HID, DM], BF16, kind="Internal").ap()
    st_wih = nc.dram_tensor("st_wih", [OC, 128, CC, 128], BF16, kind="Internal").ap()
    st_wil = nc.dram_tensor("st_wil", [OC, 128, CC, 128], BF16, kind="Internal").ap()
    st_woh = nc.dram_tensor("st_woh", [DI, DM], BF16, kind="Internal").ap()
    st_wol = nc.dram_tensor("st_wol", [DI, DM], BF16, kind="Internal").ap()
    st_kh = nc.dram_tensor("st_kh", [T, DM], BF16, kind="Internal").ap()
    st_kl = nc.dram_tensor("st_kl", [T, DM], BF16, kind="Internal").ap()
    st_vh = nc.dram_tensor("st_vh", [T, DM], BF16, kind="Internal").ap()
    st_vl = nc.dram_tensor("st_vl", [T, DM], BF16, kind="Internal").ap()

    es = contextlib.ExitStack()
    with es:
        X = es.enter_context(nc.sbuf_tensor("X", [128, TT * DM], F32))
        MRS = es.enter_context(nc.sbuf_tensor("MRS", [128, TT * 4], F32))
        iden = es.enter_context(nc.sbuf_tensor("iden_s", [128, 128], F32))
        ones1 = es.enter_context(nc.sbuf_tensor("ones1", [1, 128], F32))
        epsc = es.enter_context(nc.sbuf_tensor("epsc", [128, 1], F32))
        tcol = es.enter_context(nc.sbuf_tensor("tcol", [128, 4], F32))
        bnbuf = es.enter_context(nc.sbuf_tensor("bnbuf", [128, 16], F32))
        hbuf1 = es.enter_context(nc.sbuf_tensor("hbuf1", [128, DM], F32))
        PS = [es.enter_context(nc.psum_tensor(f"ps{i}", [128, 512], F32)) for i in range(8)]
        seq = es.enter_context(nc.semaphore("seq"))
        block = es.enter_context(nc.Block())

        sems = {"seq": [seq, 0]}
        ENG = {"tensor": "pe", "vector": "ve", "scalar": "ae", "gpsimd": "ge", "sync": "sy"}
        waited = {}

        def sem_of(name):
            if name not in sems:
                sems[name] = [es.enter_context(nc.semaphore(name)), 0]
            return sems[name]

        for _e in ENG.values():
            sem_of(_e)

        def stage(eng, fn, inc=1, auto=True, deps=None, sem=None):
            sname = sem or ENG[eng]
            s = sem_of(sname)
            dl = [(n, x[1]) for n, x in sems.items()] if deps is None else list(deps)
            own = ENG[eng]
            dl = list(dl) + [(own, sems[own][1])]
            w = waited.setdefault(eng, {})
            final = []
            for d in dl:
                if d is None:
                    continue
                n, v = d
                if v <= 0 or w.get(n, 0) >= v:
                    continue
                final.append((n, v))
                w[n] = v

            def _f(e, fn=fn, final=final, inc=inc, auto=auto, s=s):
                for n, v in final:
                    e.wait_ge(sems[n][0], v)
                last = fn(e)
                if auto:
                    last.then_inc(s[0], inc)
            getattr(block, eng)(_f)
            if auto:
                s[1] += inc
            return (sname, s[1])

        def dmas(pairs, sem="lmisc", deps=None):
            s = sem_of(sem)

            def _f(e, pairs=pairs, s=s):
                for dst, src in pairs:
                    e.dma_start(out=dst, in_=src).then_inc(s[0], 16)
            stage("sync", _f, inc=0, auto=False, deps=deps)
            s[1] += 16 * len(pairs)
            return (sem, s[1])

        def Xt(i):
            return X[:, DM * i:DM * (i + 1)]

        def emit_ln(src, hdst, save_i=None, deps=None):
            src = src[:, 0:DM]
            hdst = hdst[:, 0:DM]

            def f_bn(v):
                v.bn_stats(bnbuf[:, 0:6], src[:, 0:512])
                v.bn_stats(bnbuf[:, 6:12], src[:, 512:1024])
                v.drain()
                return v.bn_aggr(bnbuf[:, 12:14], bnbuf[:, 0:12])
            t_bn = stage("vector", f_bn, deps=deps)
            t_sq = stage("scalar", lambda s: s.activation(bnbuf[:, 14:15], bnbuf[:, 13:14], AF.Sqrt,
                                                          bias=epsc[:], scale=1.0), deps=[t_bn])

            def f_rs(v):
                v.reciprocal(tcol[:, 2:3], bnbuf[:, 14:15])
                v.tensor_copy(tcol[:, 1:2], bnbuf[:, 12:13])
                v.drain()
                return v.tensor_scalar(tcol[:, 0:1], tcol[:, 1:2], tcol[:, 2:3], -1.0,
                                       op0=OP.mult, op1=OP.mult)
            t_rs = stage("vector", f_rs, deps=[t_sq])
            if save_i is not None:
                stage("vector", lambda v: v.tensor_copy(MRS[:, 4 * save_i:4 * save_i + 2], tcol[:, 1:3]),
                      deps=[t_rs])
            return stage("scalar", lambda s: s.activation(hdst, src, AF.Identity,
                                                          scale=tcol[:, 2:3], bias=tcol[:, 0:1]),
                         deps=[t_rs])

        last_tsp = [None, None]

        def emit_transpose_split(srcf, dst_hi, dst_lo, col_of, deps=None):
            def f_tr(t):
                last = None
                for c in range(CC):
                    bank = PS[6 + c // 4]
                    last = t.transpose(bank[:, 128 * (c % 4):128 * (c % 4) + 128],
                                       srcf[:, 128 * c:128 * (c + 1)], iden[:])
                return last
            dtr = None if deps is None else (list(deps) + [t for t in last_tsp if t])
            t_tr = stage("tensor", f_tr, deps=dtr)

            def f_hi(s):
                last = None
                for c in range(CC):
                    bank = PS[6 + c // 4]
                    last = s.copy(dst_hi[:, col_of(c):col_of(c) + 128],
                                  bank[:, 128 * (c % 4):128 * (c % 4) + 128])
                return last
            t_hi = stage("scalar", f_hi, deps=[t_tr])

            def f_lo(v):
                last = None
                for c in range(CC):
                    bank = PS[6 + c // 4]
                    last = v.tensor_tensor(dst_lo[:, col_of(c):col_of(c) + 128],
                                           bank[:, 128 * (c % 4):128 * (c % 4) + 128],
                                           dst_hi[:, col_of(c):col_of(c) + 128], op=OP.subtract)
                return last
            t_lo = stage("vector", f_lo, deps=[t_hi])
            last_tsp[0], last_tsp[1] = t_hi, t_lo
            return (t_tr, t_hi, t_lo)

        def emit_mm3(t, pslice, lhsT_hi, lhsT_lo, rhs_hi, rhs_lo, nk=CC):
            last = None
            for c in range(nk):
                last = t.matmul(pslice, lhsT_hi(c), rhs_hi(c), start=(c == 0), stop=False)
                last = t.matmul(pslice, lhsT_hi(c), rhs_lo(c), start=False, stop=False)
                last = t.matmul(pslice, lhsT_lo(c), rhs_hi(c), start=False, stop=(c == nk - 1))
            return last

        def emit_bcast_row(src_row, dst, rowbuf):
            dmas([(rowbuf[:], src_row)])

            def f_mm(t):
                t.matmul(PS[0][:, 0:512], ones1[:], rowbuf[:, 0:512], start=True, stop=True)
                return t.matmul(PS[1][:, 0:512], ones1[:], rowbuf[:, 512:1024], start=True, stop=True)
            stage("tensor", f_mm)

            def f_ev(s):
                s.copy(dst[:, 0:512], PS[0][:, 0:512])
                return s.copy(dst[:, 512:1024], PS[1][:, 0:512])
            stage("scalar", f_ev)

        def emit_load_split_w(src, whi, wlo, wldf):
            for c in range(CC):
                dmas([(wldf[:], src[128 * c:128 * (c + 1), :])])

                def f(v, c=c):
                    v.tensor_copy(whi[:, DM * c:DM * (c + 1)], wldf[:])
                    v.drain()
                    return v.tensor_tensor(wlo[:, DM * c:DM * (c + 1)], wldf[:],
                                           whi[:, DM * c:DM * (c + 1)], op=OP.subtract)
                stage("vector", f)

        with contextlib.ExitStack() as p0:
            ibuf = p0.enter_context(nc.sbuf_tensor("p0_ibuf", [128, HID], I32))
            bbuf = p0.enter_context(nc.sbuf_tensor("p0_bbuf", [128, HID], BF16))
            wf = p0.enter_context(nc.sbuf_tensor("p0_wf", [128, DI2], F32))
            sbh = p0.enter_context(nc.sbuf_tensor("p0_sbh", [128, DI2], BF16))
            sbl = p0.enter_context(nc.sbuf_tensor("p0_sbl", [128, DI2], BF16))

            dmas([(Xt(i), A["x"][128 * i:128 * (i + 1), :]) for i in range(TT)])
            dmas([(iden[:], A["iden"])])
            stage("vector", lambda v: v.memset(ones1[:], 1.0))
            stage("vector", lambda v: v.memset(epsc[:], 1e-5))

            for src, dst, rows, width in ((A["wgT"], st_wg, CC, HID), (A["wuT"], st_wu, CC, HID),
                                          (A["wdT"], st_wd, HC, DM)):
                for j in range(rows):
                    dmas([(ibuf[:, 0:width], src[128 * j:128 * (j + 1), :])])
                    stage("vector", lambda v, width=width: v.tensor_copy(bbuf[:, 0:width], ibuf[:, 0:width]))
                    dmas([(dst[128 * j:128 * (j + 1), :], bbuf[:, 0:width])])

            for j in range(CC):
                dmas([(wf[:], A["winT"][128 * j:128 * (j + 1), :])])

                def f_sp(v):
                    v.tensor_copy(sbh[:], wf[:])
                    v.drain()
                    return v.tensor_tensor(sbl[:], wf[:], sbh[:], op=OP.subtract)
                stage("vector", f_sp)
                dmas([(st_wih[o, :, j, :], sbh[:, 128 * o:128 * (o + 1)]) for o in range(OC)]
                     + [(st_wil[o, :, j, :], sbl[:, 128 * o:128 * (o + 1)]) for o in range(OC)])

            for j in range(DC):
                dmas([(wf[:, 0:DM], A["woutT"][128 * j:128 * (j + 1), :])])

                def f_sp2(v):
                    v.tensor_copy(sbh[:, 0:DM], wf[:, 0:DM])
                    v.drain()
                    return v.tensor_tensor(sbl[:, 0:DM], wf[:, 0:DM], sbh[:, 0:DM], op=OP.subtract)
                stage("vector", f_sp2)
                dmas([(st_woh[128 * j:128 * (j + 1), :], sbh[:, 0:DM]),
                      (st_wol[128 * j:128 * (j + 1), :], sbl[:, 0:DM])])

        with contextlib.ExitStack() as pat:
            hbuf2 = pat.enter_context(nc.sbuf_tensor("hbuf2", [128, DM], F32))
            tsp_hi = pat.enter_context(nc.sbuf_tensor("tsp_hi", [128, DM], BF16))
            tsp_lo = pat.enter_context(nc.sbuf_tensor("tsp_lo", [128, DM], BF16))
            rowbuf = pat.enter_context(nc.sbuf_tensor("rowbuf", [1, DM], F32))
            normw_b = pat.enter_context(nc.sbuf_tensor("normw_b", [128, DM], F32))
            normb_b = pat.enter_context(nc.sbuf_tensor("normb_b", [128, DM], F32))

            emit_bcast_row(A["normw"], normw_b, rowbuf)
            emit_bcast_row(A["normb"], normb_b, rowbuf)

            def emit_h1(i, save):
                if save:
                    emit_ln(Xt(i), hbuf1, save_i=i)
                    emit_ln(hbuf1, hbuf2)
                    stage("vector", lambda v: v.tensor_copy(MRS[:, 4 * i + 2:4 * i + 4], tcol[:, 1:3]))
                else:
                    def f_b0(v):
                        return v.tensor_scalar(tcol[:, 0:1], MRS[:, 4 * i:4 * i + 1],
                                               MRS[:, 4 * i + 1:4 * i + 2], -1.0, op0=OP.mult, op1=OP.mult)
                    stage("vector", f_b0)
                    stage("scalar", lambda s: s.activation(hbuf1[:], Xt(i), AF.Identity,
                                                           scale=MRS[:, 4 * i + 1:4 * i + 2], bias=tcol[:, 0:1]))

                    def f_b2(v):
                        return v.tensor_scalar(tcol[:, 3:4], MRS[:, 4 * i + 2:4 * i + 3],
                                               MRS[:, 4 * i + 3:4 * i + 4], -1.0, op0=OP.mult, op1=OP.mult)
                    stage("vector", f_b2)
                    stage("scalar", lambda s: s.activation(hbuf2[:], hbuf1[:], AF.Identity,
                                                           scale=MRS[:, 4 * i + 3:4 * i + 4], bias=tcol[:, 3:4]))

                def f_aff(v):
                    v.tensor_tensor(hbuf1[:], hbuf2[:], normw_b[:], op=OP.mult)
                    v.drain()
                    return v.tensor_tensor(hbuf1[:], hbuf1[:], normb_b[:], op=OP.add)
                stage("vector", f_aff)

            with contextlib.ExitStack() as pa:
                bk_b = pa.enter_context(nc.sbuf_tensor("bk_b", [128, DM], F32))
                bv_b = pa.enter_context(nc.sbuf_tensor("bv_b", [128, DM], F32))
                wk_hi = pa.enter_context(nc.sbuf_tensor("wk_hi", [128, CC * DM], BF16))
                wk_lo = pa.enter_context(nc.sbuf_tensor("wk_lo", [128, CC * DM], BF16))
                wv_hi = pa.enter_context(nc.sbuf_tensor("wv_hi", [128, CC * DM], BF16))
                wv_lo = pa.enter_context(nc.sbuf_tensor("wv_lo", [128, CC * DM], BF16))
                wldf = pa.enter_context(nc.sbuf_tensor("wldf_a", [128, DM], F32))
                kv_hi = pa.enter_context(nc.sbuf_tensor("kv_hi", [128, DM], BF16))
                kv_lo = pa.enter_context(nc.sbuf_tensor("kv_lo", [128, DM], BF16))

                emit_bcast_row(A["bk"], bk_b, rowbuf)
                emit_bcast_row(A["bv"], bv_b, rowbuf)
                emit_load_split_w(A["wkT"], wk_hi, wk_lo, wldf)
                emit_load_split_w(A["wvT"], wv_hi, wv_lo, wldf)

                for i in range(TT):
                    emit_h1(i, save=True)
                    emit_transpose_split(hbuf1, tsp_hi, tsp_lo, lambda c: 128 * c)
                    for whi, wlo, bias_b, sth, stl in (
                            (wk_hi, wk_lo, bk_b, st_kh, st_kl),
                            (wv_hi, wv_lo, bv_b, st_vh, st_vl)):
                        for half in range(2):
                            def f_mm(t, whi=whi, wlo=wlo, half=half):
                                return emit_mm3(
                                    t, PS[half][:, 0:512],
                                    lambda c: tsp_hi[:, 128 * c:128 * (c + 1)],
                                    lambda c: tsp_lo[:, 128 * c:128 * (c + 1)],
                                    lambda c: whi[:, DM * c + 512 * half:DM * c + 512 * (half + 1)],
                                    lambda c: wlo[:, DM * c + 512 * half:DM * c + 512 * (half + 1)])
                            stage("tensor", f_mm)

                            def f_ev(v, half=half, bias_b=bias_b):
                                return v.tensor_tensor(hbuf2[:, 512 * half:512 * (half + 1)],
                                                       PS[half][:, 0:512],
                                                       bias_b[:, 512 * half:512 * (half + 1)], op=OP.add)
                            stage("vector", f_ev)

                        def f_sp(v):
                            v.tensor_copy(kv_hi[:], hbuf2[:])
                            v.drain()
                            return v.tensor_tensor(kv_lo[:], hbuf2[:], kv_hi[:], op=OP.subtract)
                        stage("vector", f_sp)
                        dmas([(sth[128 * i:128 * (i + 1), :], kv_hi[:]),
                              (stl[128 * i:128 * (i + 1), :], kv_lo[:])])

            with contextlib.ExitStack() as pa:
                S_hi = pa.enter_context(nc.sbuf_tensor("S_hi", [128, CC * DM], BF16))
                S_lo = pa.enter_context(nc.sbuf_tensor("S_lo", [128, CC * DM], BF16))
                with contextlib.ExitStack() as ps_:
                    kt_hi = ps_.enter_context(nc.sbuf_tensor("kt_hi", [128, DM], BF16))
                    kt_lo = ps_.enter_context(nc.sbuf_tensor("kt_lo", [128, DM], BF16))
                    vt_hi = ps_.enter_context(nc.sbuf_tensor("vt_hi", [128, DM], BF16))
                    vt_lo = ps_.enter_context(nc.sbuf_tensor("vt_lo", [128, DM], BF16))
                    for r in range(2):
                        for i in range(TT):
                            dmas([(kt_hi[:], st_kh[128 * i:128 * (i + 1), :]),
                                  (kt_lo[:], st_kl[128 * i:128 * (i + 1), :]),
                                  (vt_hi[:], st_vh[128 * i:128 * (i + 1), :]),
                                  (vt_lo[:], st_vl[128 * i:128 * (i + 1), :])])

                            def f_mm(t, i=i, r=r):
                                last = None
                                for c1 in range(CC):
                                    p = PS[c1][:, 0:512]
                                    last = t.matmul(p, kt_hi[:, 128 * c1:128 * (c1 + 1)],
                                                    vt_hi[:, 512 * r:512 * (r + 1)],
                                                    start=(i == 0), stop=False)
                                    last = t.matmul(p, kt_hi[:, 128 * c1:128 * (c1 + 1)],
                                                    vt_lo[:, 512 * r:512 * (r + 1)], start=False, stop=False)
                                    last = t.matmul(p, kt_lo[:, 128 * c1:128 * (c1 + 1)],
                                                    vt_hi[:, 512 * r:512 * (r + 1)],
                                                    start=False, stop=(i == TT - 1))
                                return last
                            stage("tensor", f_mm)

                        def f_evh(s, r=r):
                            last = None
                            for c1 in range(CC):
                                last = s.copy(S_hi[:, DM * c1 + 512 * r:DM * c1 + 512 * (r + 1)],
                                              PS[c1][:, 0:512])
                            return last
                        stage("scalar", f_evh)

                        def f_evl(v, r=r):
                            last = None
                            for c1 in range(CC):
                                last = v.tensor_tensor(S_lo[:, DM * c1 + 512 * r:DM * c1 + 512 * (r + 1)],
                                                       PS[c1][:, 0:512],
                                                       S_hi[:, DM * c1 + 512 * r:DM * c1 + 512 * (r + 1)],
                                                       op=OP.subtract)
                            return last
                        stage("vector", f_evl)

                with contextlib.ExitStack() as pb:
                    bq_b = pb.enter_context(nc.sbuf_tensor("bq_b", [128, DM], F32))
                    bp_b = pb.enter_context(nc.sbuf_tensor("bp_b", [128, DM], F32))
                    wq_hi = pb.enter_context(nc.sbuf_tensor("wq_hi", [128, CC * DM], BF16))
                    wq_lo = pb.enter_context(nc.sbuf_tensor("wq_lo", [128, CC * DM], BF16))
                    wp_hi = pb.enter_context(nc.sbuf_tensor("wp_hi", [128, CC * DM], BF16))
                    wp_lo = pb.enter_context(nc.sbuf_tensor("wp_lo", [128, CC * DM], BF16))
                    wldf = pb.enter_context(nc.sbuf_tensor("wldf_b", [128, DM], F32))

                    emit_bcast_row(A["bq"], bq_b, rowbuf)
                    emit_bcast_row(A["bp"], bp_b, rowbuf)
                    emit_load_split_w(A["wqT"], wq_hi, wq_lo, wldf)
                    emit_load_split_w(A["wpT"], wp_hi, wp_lo, wldf)

                    def mm_pair(bank_base, whi, wlo, evac):
                        for half in range(2):
                            def f_mm(t, half=half):
                                return emit_mm3(
                                    t, PS[bank_base + half][:, 0:512],
                                    lambda c: tsp_hi[:, 128 * c:128 * (c + 1)],
                                    lambda c: tsp_lo[:, 128 * c:128 * (c + 1)],
                                    lambda c: whi[:, DM * c + 512 * half:DM * c + 512 * (half + 1)],
                                    lambda c: wlo[:, DM * c + 512 * half:DM * c + 512 * (half + 1)])
                            stage("tensor", f_mm)
                            evac(half)

                    for i in range(TT):
                        emit_h1(i, save=False)
                        emit_transpose_split(hbuf1, tsp_hi, tsp_lo, lambda c: 128 * c)

                        def ev_q(half):
                            def f(v, half=half):
                                return v.tensor_tensor(hbuf2[:, 512 * half:512 * (half + 1)],
                                                       PS[half][:, 0:512],
                                                       bq_b[:, 512 * half:512 * (half + 1)], op=OP.add)
                            stage("vector", f)
                        mm_pair(0, wq_hi, wq_lo, ev_q)
                        emit_transpose_split(hbuf2, tsp_hi, tsp_lo, lambda c: 128 * c)

                        def ev_o1(half):
                            stage("scalar", lambda s, half=half: s.activation(
                                hbuf1[:, 512 * half:512 * (half + 1)], PS[2 + half][:, 0:512],
                                AF.Identity, scale=float(1.0 / 32.0)))
                        for half in range(2):
                            def f_mm(t, half=half):
                                return emit_mm3(
                                    t, PS[2 + half][:, 0:512],
                                    lambda c: tsp_hi[:, 128 * c:128 * (c + 1)],
                                    lambda c: tsp_lo[:, 128 * c:128 * (c + 1)],
                                    lambda c: S_hi[:, DM * c + 512 * half:DM * c + 512 * (half + 1)],
                                    lambda c: S_lo[:, DM * c + 512 * half:DM * c + 512 * (half + 1)])
                            stage("tensor", f_mm)
                            ev_o1(half)
                        emit_transpose_split(hbuf1, tsp_hi, tsp_lo, lambda c: 128 * c)

                        def ev_p(half):
                            def f(v, half=half):
                                return v.tensor_tensor(hbuf2[:, 512 * half:512 * (half + 1)],
                                                       PS[4 + half][:, 0:512],
                                                       bp_b[:, 512 * half:512 * (half + 1)], op=OP.add)
                            stage("vector", f)
                        for half in range(2):
                            def f_mm(t, half=half):
                                return emit_mm3(
                                    t, PS[4 + half][:, 0:512],
                                    lambda c: tsp_hi[:, 128 * c:128 * (c + 1)],
                                    lambda c: tsp_lo[:, 128 * c:128 * (c + 1)],
                                    lambda c: wp_hi[:, DM * c + 512 * half:DM * c + 512 * (half + 1)],
                                    lambda c: wp_lo[:, DM * c + 512 * half:DM * c + 512 * (half + 1)])
                            stage("tensor", f_mm)
                            ev_p(half)

                        def f_b0(v, i=i):
                            return v.tensor_scalar(tcol[:, 0:1], MRS[:, 4 * i:4 * i + 1],
                                                   MRS[:, 4 * i + 1:4 * i + 2], -1.0,
                                                   op0=OP.mult, op1=OP.mult)
                        stage("vector", f_b0)
                        stage("scalar", lambda s, i=i: s.activation(hbuf1[:], Xt(i), AF.Identity,
                                                                    scale=MRS[:, 4 * i + 1:4 * i + 2],
                                                                    bias=tcol[:, 0:1]))

                        def f_res(v, i=i):
                            v.tensor_tensor(hbuf2[:], hbuf2[:], hbuf1[:], op=OP.add)
                            v.drain()
                            return v.tensor_tensor(Xt(i), Xt(i), hbuf2[:], op=OP.add)
                        stage("vector", f_res)

            if debug:
                dmas([(dbg["dbg_attn"][128 * i:128 * (i + 1), :], Xt(i)) for i in range(TT)])

        def emit_ffn(tag):
            with contextlib.ExitStack() as pf:
                xlnT_hi = pf.enter_context(nc.sbuf_tensor(f"xlnT_hi_{tag}", [128, CC * T], BF16))
                xlnT_lo = pf.enter_context(nc.sbuf_tensor(f"xlnT_lo_{tag}", [128, CC * T], BF16))
                wg_q = pf.enter_context(nc.sbuf_tensor(f"wg_q_{tag}", [128, CC * 768], BF16))
                wu_q = pf.enter_context(nc.sbuf_tensor(f"wu_q_{tag}", [128, CC * 768], BF16))
                wd_q = pf.enter_context(nc.sbuf_tensor(f"wd_q_{tag}", [128, 6 * DM], BF16))
                sg = pf.enter_context(nc.sbuf_tensor(f"sg_{tag}", [128, 256], F32))
                hhf = pf.enter_context(nc.sbuf_tensor(f"hhf_{tag}", [128, 256], F32))
                hh_hi = pf.enter_context(nc.sbuf_tensor(f"hh_hi_{tag}", [128, 6 * 256], BF16))
                hh_lo = pf.enter_context(nc.sbuf_tensor(f"hh_lo_{tag}", [128, 6 * 256], BF16))

                for i in range(TT):
                    emit_ln(Xt(i), hbuf1)
                    emit_transpose_split(hbuf1, xlnT_hi, xlnT_lo, lambda c, i=i: T * c + 128 * i)

                for q in range(4):
                    dmas([(wg_q[:].rearrange("p (c h) -> p c h", h=768),
                           st_wg.rearrange("(c p) h -> p c h", p=128)[:, :, 768 * q:768 * (q + 1)]),
                          (wu_q[:].rearrange("p (c h) -> p c h", h=768),
                           st_wu.rearrange("(c p) h -> p c h", p=128)[:, :, 768 * q:768 * (q + 1)])])
                    dmas([(wd_q[:, DM * h:DM * (h + 1)],
                           st_wd[128 * (6 * q + h):128 * (6 * q + h + 1), :]) for h in range(6)])

                    for sl in range(NSC):
                        t0 = 256 * sl
                        for h in range(6):
                            def f_g(t, h=h, t0=t0):
                                last = None
                                for wq_, bank in ((wg_q, PS[0]), (wu_q, PS[1])):
                                    for c in range(CC):
                                        last = t.matmul(bank[:, 0:256],
                                                        wq_[:, 768 * c + 128 * h:768 * c + 128 * (h + 1)],
                                                        xlnT_hi[:, T * c + t0:T * c + t0 + 256],
                                                        start=(c == 0), stop=False)
                                        last = t.matmul(bank[:, 0:256],
                                                        wq_[:, 768 * c + 128 * h:768 * c + 128 * (h + 1)],
                                                        xlnT_lo[:, T * c + t0:T * c + t0 + 256],
                                                        start=False, stop=(c == CC - 1))
                                return last
                            stage("tensor", f_g)
                            stage("scalar", lambda s: s.activation(sg[:], PS[0][:, 0:256],
                                                                   AF.Silu, scale=S127))

                            def f_hh(v, h=h):
                                v.tensor_tensor(hhf[:], sg[:], PS[1][:, 0:256], op=OP.mult)
                                v.drain()
                                v.tensor_copy(hh_hi[:, 256 * h:256 * (h + 1)], hhf[:])
                                v.drain()
                                return v.tensor_tensor(hh_lo[:, 256 * h:256 * (h + 1)], hhf[:],
                                                       hh_hi[:, 256 * h:256 * (h + 1)], op=OP.subtract)
                            stage("vector", f_hh)

                            def f_d(t, h=h):
                                last = None
                                for tsub in range(2):
                                    for ch in range(2):
                                        bank = PS[2 + 2 * tsub + ch]
                                        last = t.matmul(
                                            bank[:, 0:512],
                                            hh_hi[:, 256 * h + 128 * tsub:256 * h + 128 * (tsub + 1)],
                                            wd_q[:, DM * h + 512 * ch:DM * h + 512 * (ch + 1)],
                                            start=(h == 0), stop=False)
                                        last = t.matmul(
                                            bank[:, 0:512],
                                            hh_lo[:, 256 * h + 128 * tsub:256 * h + 128 * (tsub + 1)],
                                            wd_q[:, DM * h + 512 * ch:DM * h + 512 * (ch + 1)],
                                            start=False, stop=(h == 5))
                                return last
                            stage("tensor", f_d)

                        def f_acc(v, sl=sl):
                            last = None
                            for tsub in range(2):
                                i = 2 * sl + tsub
                                for ch in range(2):
                                    bank = PS[2 + 2 * tsub + ch]
                                    xs = X[:, DM * i + 512 * ch:DM * i + 512 * (ch + 1)]
                                    last = v.scalar_tensor_tensor(xs, bank[:, 0:512], S127SQ, xs,
                                                                  op0=OP.mult, op1=OP.add)
                            return last
                        stage("vector", f_acc)

                if debug and tag == "f1":
                    dmas([(dbg["dbg_ffn1"][128 * i:128 * (i + 1), :], Xt(i)) for i in range(TT)])

        emit_ffn("f1")

        with contextlib.ExitStack() as pm:
            A_r = pm.enter_context(nc.sbuf_tensor("A_r", [128, DC * NST], F32))
            D_r = pm.enter_context(nc.sbuf_tensor("D_r", [128, DC], F32))
            bconv_r = pm.enter_context(nc.sbuf_tensor("bconv_r", [128, DC], F32))
            bdt_r = pm.enter_context(nc.sbuf_tensor("bdt_r", [128, DC], F32))
            wconv_r = pm.enter_context(nc.sbuf_tensor("wconv_r", [128, DC * 4], F32))
            wxr_s = pm.enter_context(nc.sbuf_tensor("wxr_s", [128, DC * 96], F32))
            wdtT_s = pm.enter_context(nc.sbuf_tensor("wdtT_s", [DTR, DI], F32))
            sel_s = pm.enter_context(nc.sbuf_tensor("sel_s", [NST, NST * 128], F32))
            xcb = pm.enter_context(nc.sbuf_tensor("xcb", [128, DC * 259], F32))
            ub = pm.enter_context(nc.sbuf_tensor("ub", [128, DC * 256], F32))
            deltab = pm.enter_context(nc.sbuf_tensor("deltab", [128, DC * 256], F32))
            gb = pm.enter_context(nc.sbuf_tensor("gb", [128, DC * 256], F32))
            sres = pm.enter_context(nc.sbuf_tensor("sres", [128, DC * 256], F32))
            halob = pm.enter_context(nc.sbuf_tensor("halob", [128, DC * 3], F32))
            scan_init = pm.enter_context(nc.sbuf_tensor("scan_init", [128, DC * NST], F32))
            Bb = pm.enter_context(nc.sbuf_tensor("Bb", [128, 2 * 256], F32))
            Cb = pm.enter_context(nc.sbuf_tensor("Cb", [128, 2 * 256], F32))
            dt_sb = pm.enter_context(nc.sbuf_tensor("dt_sb", [DTR, 256], F32))
            B_sb = pm.enter_context(nc.sbuf_tensor("B_sb", [NST, 256], F32))
            C_sb = pm.enter_context(nc.sbuf_tensor("C_sb", [NST, 256], F32))
            z1 = pm.enter_context(nc.sbuf_tensor("z1", [128, 2 * 256], F32))
            wob_hi = pm.enter_context(nc.sbuf_tensor("wob_hi", [128, 2 * DM], BF16))
            wob_lo = pm.enter_context(nc.sbuf_tensor("wob_lo", [128, 2 * DM], BF16))

            dmas([(A_r[:], A["alog"]), (D_r[:], A["dvec"]), (bconv_r[:], A["bconv"]),
                  (bdt_r[:], A["bdt"]), (wconv_r[:], A["wconv"]), (wxr_s[:], A["wxr"]),
                  (wdtT_s[:], A["wdtT"]), (sel_s[:], A["sel"])])
            stage("scalar", lambda s: s.activation(A_r[:], A_r[:], AF.Exp, scale=1.0))
            stage("vector", lambda v: v.tensor_scalar(A_r[:], A_r[:], -1.0, None, op0=OP.mult))
            stage("vector", lambda v: v.memset(halob[:], 0.0))
            stage("vector", lambda v: v.memset(scan_init[:], 0.0))

            xcb3 = xcb[:].rearrange("p (c k) -> p c k", k=259)
            dA = xcb[:, 0:DC * 129]
            dBu = xcb[:, DC * 129:DC * 258]
            dA3 = dA.rearrange("p (c k) -> p c k", k=129)
            dBu3 = dBu.rearrange("p (c k) -> p c k", k=129)
            si3 = scan_init[:].rearrange("p (c n) -> p c n", n=NST)
            gb3 = gb[:].rearrange("p (c k) -> p c k", k=256)
            ub3 = ub[:].rearrange("p (c k) -> p c k", k=256)

            fresh = [None]

            def mstage(eng, fn, deps, **kw):
                if eng not in fresh[0]:
                    fresh[0].add(eng)
                    return stage(eng, fn, deps=None, **kw)
                return stage(eng, fn, deps=[d for d in deps if d], **kw)

            def mdmas(pairs, sem, deps):
                if "sync" not in fresh[0]:
                    fresh[0].add("sync")
                    return dmas(pairs, sem=sem, deps=None)
                return dmas(pairs, sem=sem, deps=[d for d in deps if d])

            for sc in range(NSC):
                i0 = 2 * sc
                fresh[0] = set()
                bfree = {b: [] for b in range(8)}

                pfr = contextlib.ExitStack()
                xlnT_hi = pfr.enter_context(nc.sbuf_tensor(f"xlnT_hi_m{sc}", [128, CC * 256], BF16))
                xlnT_lo = pfr.enter_context(nc.sbuf_tensor(f"xlnT_lo_m{sc}", [128, CC * 256], BF16))
                wib_hi = pfr.enter_context(nc.sbuf_tensor(f"wib_hi{sc}", [128, 2 * CC * 128], BF16))
                wib_lo = pfr.enter_context(nc.sbuf_tensor(f"wib_lo{sc}", [128, 2 * CC * 128], BF16))

                t_lo_all = []
                for tl in range(2):
                    dep0 = None if tl == 0 else [t_lo_all[-1]]
                    t_ap = emit_ln(Xt(i0 + tl), hbuf1, deps=dep0)
                    fresh[0].update(("vector", "scalar"))
                    ttr, thi, tlo = emit_transpose_split(hbuf1, xlnT_hi, xlnT_lo,
                                                         lambda c, tl=tl: 256 * c + 128 * tl,
                                                         deps=[t_ap])
                    fresh[0].add("tensor")
                    t_lo_all.append(tlo)
                bfree[6] = [thi, tlo]
                bfree[7] = [thi, tlo]
                t_xln = [thi, tlo]

                t_wimm = [None] * OC
                t_wevac = [None] * OC
                t_wld = [None] * OC
                for o in range(OC):
                    sl = o % 2
                    t_wld[o] = mdmas(
                        [(wib_hi[:, sl * DM:sl * DM + DM].rearrange("p (c k) -> p c k", k=128), st_wih[o]),
                         (wib_lo[:, sl * DM:sl * DM + DM].rearrange("p (c k) -> p c k", k=128), st_wil[o])],
                        sem=f"lwi{sl}", deps=[t_wimm[o - 2] if o >= 2 else None])
                    bank = o % 4

                    def f_mm(t, o=o, sl=sl, bank=bank):
                        return emit_mm3(
                            t, PS[bank][:, 0:256],
                            lambda c: wib_hi[:, sl * DM + 128 * c:sl * DM + 128 * (c + 1)],
                            lambda c: wib_lo[:, sl * DM + 128 * c:sl * DM + 128 * (c + 1)],
                            lambda c: xlnT_hi[:, 256 * c:256 * (c + 1)],
                            lambda c: xlnT_lo[:, 256 * c:256 * (c + 1)])
                    t_wimm[o] = mstage("tensor", f_mm,
                                       deps=[t_wld[o]] + t_xln + bfree[bank])
                    if o < DC:
                        t_wevac[o] = mstage("scalar", lambda s, o=o, bank=bank: s.copy(
                            xcb[:, 259 * o + 3:259 * (o + 1)], PS[bank][:, 0:256]),
                            deps=[t_wimm[o]])
                    else:
                        t_wevac[o] = mstage("scalar", lambda s, o=o, bank=bank: s.activation(
                            sres[:, 256 * (o - DC):256 * (o - DC + 1)], PS[bank][:, 0:256], AF.Silu),
                            deps=[t_wimm[o]])
                    bfree[bank] = [t_wevac[o]]

                t_hin = mstage("vector", lambda v: v.tensor_copy(
                    xcb3[:, :, 0:3], halob[:].rearrange("p (c k) -> p c k", k=3)),
                    deps=[t_wevac[DC - 1]])
                t_cv = None
                for j in range(4):
                    def f_cv(v, j=j):
                        last = None
                        for c in range(DC):
                            acc = ub[:, 256 * c:256 * (c + 1)]
                            xs = xcb[:, 259 * c + j:259 * c + j + 256]
                            wj = wconv_r[:, 4 * c + j:4 * c + j + 1]
                            if j == 0:
                                last = v.tensor_scalar(acc, xs, wj, None, op0=OP.mult)
                            else:
                                last = v.scalar_tensor_tensor(acc, xs, wj, acc, op0=OP.mult, op1=OP.add)
                        return last
                    t_cv = mstage("vector", f_cv,
                                  deps=[t_hin, t_wevac[DC - 1], ("ve", sems["ve"][1])])
                t_hsv = mstage("vector", lambda v: v.tensor_copy(
                    halob[:].rearrange("p (c k) -> p c k", k=3), xcb3[:, :, 256:259]),
                    deps=[t_cv])

                def f_u(s):
                    last = None
                    for c in range(DC):
                        last = s.activation(ub[:, 256 * c:256 * (c + 1)], ub[:, 256 * c:256 * (c + 1)],
                                            AF.Silu, bias=bconv_r[:, c:c + 1], scale=1.0)
                    return last
                t_u = mstage("scalar", f_u, deps=[t_cv])

                def f_xp(t):
                    last = None
                    for lo, hi, bank, rows in ((0, 64, PS[4], DTR), (64, 80, PS[5], NST), (80, 96, PS[6], NST)):
                        for c in range(DC):
                            last = t.matmul(bank[0:rows, 0:256], wxr_s[:, 96 * c + lo:96 * c + hi],
                                            ub[:, 256 * c:256 * (c + 1)],
                                            start=(c == 0), stop=(c == DC - 1))
                    return last
                t_xp = mstage("tensor", f_xp, deps=[t_u] + bfree[4] + bfree[5] + bfree[6])

                def f_xpe(s):
                    s.copy(dt_sb[:], PS[4][0:DTR, 0:256])
                    s.copy(B_sb[:], PS[5][0:NST, 0:256])
                    return s.copy(C_sb[:], PS[6][0:NST, 0:256])
                t_xpe = mstage("scalar", f_xpe, deps=[t_xp])
                bfree[4] = bfree[5] = bfree[6] = [t_xpe]

                t_dmm = [None] * DC
                t_fm = [None] * DC
                t_abs = [None] * DC
                t_fadd = [None] * DC
                for c in range(DC):
                    bank = c % 4
                    t_dmm[c] = mstage("tensor", lambda t, c=c, bank=bank: t.matmul(
                        PS[bank][:, 0:256], wdtT_s[:, 128 * c:128 * (c + 1)], dt_sb[:],
                        start=True, stop=True),
                        deps=[t_xpe] + bfree[bank])
                    t_fm[c] = mstage("vector", lambda v, c=c, bank=bank: v.tensor_scalar(
                        deltab[:, 256 * c:256 * (c + 1)], PS[bank][:, 0:256],
                        bdt_r[:, c:c + 1], 0.0, op0=OP.add, op1=OP.max),
                        deps=[t_dmm[c]])
                    zsl = z1[:, 256 * (c % 2):256 * (c % 2) + 256]
                    t_abs[c] = mstage("scalar", lambda s, c=c, bank=bank, zsl=zsl: s.activation(
                        zsl, PS[bank][:, 0:256], AF.Abs, bias=bdt_r[:, c:c + 1], scale=1.0),
                        deps=[t_dmm[c], t_fadd[c - 2] if c >= 2 else None])
                    bfree[bank] = [t_fm[c], t_abs[c]]
                    t_e = mstage("scalar", lambda s, zsl=zsl: s.activation(zsl, zsl, AF.Exp, scale=-1.0),
                                 deps=[t_abs[c]])
                    t_l = mstage("scalar", lambda s, zsl=zsl: s.activation(zsl, zsl, AF.Ln, bias=1.0, scale=1.0),
                                 deps=[t_e])
                    t_fadd[c] = mstage("vector", lambda v, c=c, zsl=zsl: v.tensor_tensor(
                        deltab[:, 256 * c:256 * (c + 1)], deltab[:, 256 * c:256 * (c + 1)], zsl, op=OP.add),
                        deps=[t_l])

                t_g = mstage("vector", lambda v: v.tensor_tensor(gb[:], deltab[:], ub[:], op=OP.mult),
                             deps=[t_u, t_fadd[DC - 1]])

                def f_ud(v):
                    last = None
                    for c in range(DC):
                        last = v.tensor_scalar(ub[:, 256 * c:256 * (c + 1)], ub[:, 256 * c:256 * (c + 1)],
                                               D_r[:, c:c + 1], None, op0=OP.mult)
                    return last
                t_ud = mstage("vector", f_ud, deps=[t_xp, t_g])
                pfr.close()

                pn = contextlib.ExitStack()
                tmpb = pn.enter_context(nc.sbuf_tensor(f"tmpb_{sc}", [128, DC * 128], F32))
                tmpb3 = tmpb[:].rearrange("p (c k) -> p c k", k=128)
                A3 = A_r[:].rearrange("p (c n) -> p c n", n=NST)
                dAb = dA3[:, :, 1:129]
                dBub = dBu3[:, :, 1:129]
                t_ms = mstage("vector", lambda v: v.memset(dA3[:, :, 0:1], 0.0), deps=[t_hsv])
                t_scan = {0: t_hsv, 1: t_hsv}
                t_ymul = {0: None, 1: None}
                t_yadd = {0: None, 1: None}
                t_bce = [None, None]
                for n in range(NST):
                    slot = n % 2
                    pbank = 6 + slot

                    def f_bc(t, n=n, pbank=pbank):
                        t.matmul(PS[pbank][:, 0:256], sel_s[:, 128 * n:128 * (n + 1)],
                                 B_sb[:], start=True, stop=True)
                        return t.matmul(PS[pbank][:, 256:512], sel_s[:, 128 * n:128 * (n + 1)],
                                        C_sb[:], start=True, stop=True)
                    t_bcast = mstage("tensor", f_bc, deps=[t_xpe, t_bce[slot]] + bfree[pbank])
                    bfree[pbank] = []

                    def f_bce(s, slot=slot, pbank=pbank):
                        s.copy(Bb[:, 256 * slot:256 * (slot + 1)], PS[pbank][:, 0:256])
                        return s.copy(Cb[:, 256 * slot:256 * (slot + 1)], PS[pbank][:, 256:512])
                    t_bce[slot] = mstage("scalar", f_bce,
                                         deps=[t_bcast, t_ymul[0], t_ymul[1]])

                    for sub in range(2):
                        def f_dam(v, n=n, sub=sub):
                            dv = deltab[:].rearrange("p (c k) -> p c k", k=256)[:, :, 128 * sub:128 * (sub + 1)]
                            av = A3[:, :, n:n + 1].broadcast_to([128, DC, 128])
                            return v.tensor_tensor(dAb, dv, av, op=OP.mult)
                        t_dam = mstage("vector", f_dam,
                                       deps=[t_scan[sub], t_fadd[DC - 1], t_ms])

                        t_dae = mstage("scalar", lambda s: s.activation(dAb, dAb, AF.Exp, scale=1.0),
                                       deps=[t_dam])

                        def f_dbu(g_, n=n, sub=sub, slot=slot):
                            gv = gb3[:, :, 128 * sub:128 * (sub + 1)]
                            bv_ = Bb[:, 256 * slot + 128 * sub:256 * slot + 128 * (sub + 1)]
                            bv_ = bv_.unsqueeze(1).broadcast_to([128, DC, 128])
                            return g_.tensor_tensor(dBub, gv, bv_, op=OP.mult)
                        t_dbu = mstage("gpsimd", f_dbu,
                                       deps=[t_bce[slot], t_g, t_scan[sub], t_hsv])

                        def f_scan(v, n=n, sub=sub):
                            v.tensor_copy(dBu3[:, :, 0:1], si3[:, :, n:n + 1])
                            v.drain()
                            v.tensor_tensor_scan(dBu, dA, dBu, initial=0.0, op0=OP.mult, op1=OP.add)
                            v.drain()
                            return v.tensor_copy(si3[:, :, n:n + 1], dBu3[:, :, 128:129])
                        t_scan[sub] = mstage("vector", f_scan, deps=[t_dae, t_dbu])

                        def f_ym(g_, slot=slot, sub=sub):
                            cv = Cb[:, 256 * slot + 128 * sub:256 * slot + 128 * (sub + 1)]
                            cv = cv.unsqueeze(1).broadcast_to([128, DC, 128])
                            return g_.tensor_tensor(tmpb3, dBub, cv, op=OP.mult)
                        t_ymul[sub] = mstage("gpsimd", f_ym, deps=[t_scan[sub], t_yadd[sub]])

                        def f_ya(v, sub=sub):
                            yv = ub3[:, :, 128 * sub:128 * (sub + 1)]
                            return v.tensor_tensor(yv, yv, tmpb3, op=OP.add)
                        t_yadd[sub] = mstage("vector", f_ya, deps=[t_ymul[sub], t_ud])
                pn.close()

                py_ = contextlib.ExitStack()
                ysp_hi = py_.enter_context(nc.sbuf_tensor(f"ysp_hi_{sc}", [128, DC * 256], BF16))
                ysp_lo = py_.enter_context(nc.sbuf_tensor(f"ysp_lo_{sc}", [128, DC * 256], BF16))
                t_yf = mstage("vector", lambda v: v.tensor_tensor(ub[:], ub[:], sres[:], op=OP.mult),
                              deps=[t_yadd[0], t_yadd[1], t_wevac[OC - 1],
                                    t_ymul[0], t_ymul[1]])

                def f_ys(v):
                    v.tensor_copy(ysp_hi[:], ub[:])
                    v.drain()
                    return v.tensor_tensor(ysp_lo[:], ub[:], ysp_hi[:], op=OP.subtract)
                t_ys = mstage("vector", f_ys, deps=[t_yf])

                t_womm = [None] * DC
                t_wold = [None] * DC
                for d in range(DC):
                    sl = d % 2
                    t_wold[d] = mdmas(
                        [(wob_hi[:, sl * DM:sl * DM + DM], st_woh[128 * d:128 * (d + 1), :]),
                         (wob_lo[:, sl * DM:sl * DM + DM], st_wol[128 * d:128 * (d + 1), :])],
                        sem=f"lwo{sl}", deps=[t_womm[d - 2] if d >= 2 else None])

                    def f_wo(t, d=d, sl=sl):
                        last = None
                        for sub in range(2):
                            yh = ysp_hi[:, 256 * d + 128 * sub:256 * d + 128 * (sub + 1)]
                            yl = ysp_lo[:, 256 * d + 128 * sub:256 * d + 128 * (sub + 1)]
                            for ch in range(2):
                                bank = PS[2 * sub + ch]
                                wh = wob_hi[:, sl * DM + 512 * ch:sl * DM + 512 * (ch + 1)]
                                wl = wob_lo[:, sl * DM + 512 * ch:sl * DM + 512 * (ch + 1)]
                                last = t.matmul(bank[:, 0:512], yh, wh, start=(d == 0), stop=False)
                                last = t.matmul(bank[:, 0:512], yh, wl, start=False, stop=False)
                                last = t.matmul(bank[:, 0:512], yl, wh, start=False, stop=(d == DC - 1))
                        return last
                    deps = [t_ys, t_wold[d]]
                    if d == 0:
                        deps += bfree[0] + bfree[1] + bfree[2] + bfree[3]
                    t_womm[d] = mstage("tensor", f_wo, deps=deps)

                def f_xa(v, i0=i0):
                    last = None
                    for sub in range(2):
                        i = i0 + sub
                        for ch in range(2):
                            bank = PS[2 * sub + ch]
                            xs = X[:, DM * i + 512 * ch:DM * i + 512 * (ch + 1)]
                            last = v.scalar_tensor_tensor(xs, bank[:, 0:512], 0.0, xs,
                                                          op0=OP.add, op1=OP.add)
                    return last
                t_xa = mstage("vector", f_xa, deps=[t_womm[DC - 1]])
                for b in range(4):
                    bfree[b] = [t_xa]
                py_.close()

            if debug:
                dmas([(dbg["dbg_mamba"][128 * i:128 * (i + 1), :], Xt(i)) for i in range(TT)])

        emit_ffn("f2")

        dmas([(out[128 * i:128 * (i + 1), :], Xt(i)) for i in range(TT)])
        stage("sync", lambda e: None, inc=0, auto=False)

    return nc


def kernel(**inputs):
    T = 2048
    x = np.ascontiguousarray(inputs["L0_attn_a_x"].astype(np.float32, copy=False))
    w = _prep_weights(inputs)
    nc = build(T=T, debug=False)
    in_maps = []
    for c in range(NCORE):
        m = dict(w)
        m["x"] = np.ascontiguousarray(x[c])
        in_maps.append(m)
    res = run_bass_kernel_spmd(nc, in_maps, core_ids=list(range(NCORE)))
    return np.stack([np.asarray(res.results[c]["out"]) for c in range(NCORE)], axis=0)


# version 06
# speedup vs baseline: 1.0688x; 1.0688x over previous
import contextlib
import numpy as np

import concourse.bass as bass
import concourse.mybir as mybir
from concourse.alu_op_type import AluOpType as OP
from concourse.bass_utils import run_bass_kernel_spmd

F32 = mybir.dt.float32
BF16 = mybir.dt.bfloat16
I32 = mybir.dt.int32
AF = mybir.ActivationFunctionType

DM = 1024
DI = 2048
DI2 = 4096
NST = 16
DTR = 64
HID = 3072
NCORE = 8
CC = DM // 128
DC = DI // 128
OC = DI2 // 128
HC = HID // 128
S127 = float(np.float32(1.0 / 127.0))
S127SQ = float(np.float32(1.0 / 127.0) * np.float32(1.0 / 127.0))


def _prep_weights(inp):
    c = np.ascontiguousarray
    w = {}
    qkv = inp["L0_attn_a_qkv_w"]
    w["wqT"] = c(qkv[0:DM].T)
    w["wkT"] = c(qkv[DM:2 * DM].T)
    w["wvT"] = c(qkv[2 * DM:3 * DM].T)
    qb = inp["L0_attn_a_qkv_b"]
    w["bq"] = c(qb[None, 0:DM])
    w["bk"] = c(qb[None, DM:2 * DM])
    w["bv"] = c(qb[None, 2 * DM:3 * DM])
    w["wpT"] = c(inp["L0_attn_a_proj_w"].T)
    w["bp"] = c(inp["L0_attn_a_proj_b"][None, :])
    w["normw"] = c(inp["L0_attn_a_norm_w"][None, :])
    w["normb"] = c(inp["L0_attn_a_norm_b"][None, :])
    w["wgT"] = c(inp["L0_ffn_w_gate"].T)
    w["wuT"] = c(inp["L0_ffn_w_up"].T)
    w["wdT"] = c(inp["L0_ffn_w_down"].T)
    w["winT"] = c(inp["L0_attn_b_W_in"].T)
    w["woutT"] = c(inp["L0_attn_b_W_out"].T)
    w["wxr"] = c(inp["L0_attn_b_W_xproj"].T.reshape(DC, 128, 96).transpose(1, 0, 2).reshape(128, DC * 96))
    w["wdtT"] = c(inp["L0_attn_b_W_dt"].T)
    r = lambda v: c(v.reshape(DC, 128).T)
    w["bconv"] = r(inp["L0_attn_b_b_conv"])
    w["bdt"] = r(inp["L0_attn_b_b_dt"])
    w["dvec"] = r(inp["L0_attn_b_D"])
    w["wconv"] = c(inp["L0_attn_b_W_conv"].reshape(DI, 4).reshape(DC, 128, 4).transpose(1, 0, 2).reshape(128, DC * 4))
    w["alog"] = c(inp["L0_attn_b_A_log"].reshape(DC, 128, NST).transpose(1, 0, 2).reshape(128, DC * NST))
    w["iden"] = np.eye(128, dtype=np.float32)
    sel = np.zeros((NST, NST * 128), np.float32)
    for n in range(NST):
        sel[n, 128 * n:128 * (n + 1)] = 1.0
    w["sel"] = sel
    return w


def build(T=2048, debug=False):
    TT = T // 128
    NSC = T // 256
    nc = bass.Bass("TRN2", target_bir_lowering=False)

    dt_in = {
        "x": ([T, DM], F32), "wqT": ([DM, DM], F32), "wkT": ([DM, DM], F32),
        "wvT": ([DM, DM], F32), "bq": ([1, DM], F32), "bk": ([1, DM], F32),
        "bv": ([1, DM], F32), "wpT": ([DM, DM], F32), "bp": ([1, DM], F32),
        "normw": ([1, DM], F32), "normb": ([1, DM], F32),
        "wgT": ([DM, HID], I32), "wuT": ([DM, HID], I32), "wdT": ([HID, DM], I32),
        "winT": ([DM, DI2], F32), "woutT": ([DI, DM], F32),
        "wxr": ([128, DC * 96], F32), "wdtT": ([DTR, DI], F32),
        "bconv": ([128, DC], F32), "bdt": ([128, DC], F32), "dvec": ([128, DC], F32),
        "wconv": ([128, DC * 4], F32), "alog": ([128, DC * NST], F32),
        "iden": ([128, 128], F32), "sel": ([NST, NST * 128], F32),
    }
    A = {k: nc.dram_tensor(k, sh, dt, kind="ExternalInput").ap() for k, (sh, dt) in dt_in.items()}
    out = nc.dram_tensor("out", [T, DM], F32, kind="ExternalOutput").ap()
    dbg = {}
    if debug:
        for k in ("dbg_attn", "dbg_ffn1", "dbg_mamba"):
            dbg[k] = nc.dram_tensor(k, [T, DM], F32, kind="ExternalOutput").ap()

    st_wg = nc.dram_tensor("st_wg", [DM, HID], BF16, kind="Internal").ap()
    st_wu = nc.dram_tensor("st_wu", [DM, HID], BF16, kind="Internal").ap()
    st_wd = nc.dram_tensor("st_wd", [HID, DM], BF16, kind="Internal").ap()
    st_wih = nc.dram_tensor("st_wih", [OC, 128, CC, 128], BF16, kind="Internal").ap()
    st_wil = nc.dram_tensor("st_wil", [OC, 128, CC, 128], BF16, kind="Internal").ap()
    st_woh = nc.dram_tensor("st_woh", [DI, DM], BF16, kind="Internal").ap()
    st_wol = nc.dram_tensor("st_wol", [DI, DM], BF16, kind="Internal").ap()
    st_kh = nc.dram_tensor("st_kh", [T, DM], BF16, kind="Internal").ap()
    st_kl = nc.dram_tensor("st_kl", [T, DM], BF16, kind="Internal").ap()
    st_vh = nc.dram_tensor("st_vh", [T, DM], BF16, kind="Internal").ap()
    st_vl = nc.dram_tensor("st_vl", [T, DM], BF16, kind="Internal").ap()

    es = contextlib.ExitStack()
    with es:
        X = es.enter_context(nc.sbuf_tensor("X", [128, TT * DM], F32))
        MRS = es.enter_context(nc.sbuf_tensor("MRS", [128, TT * 4], F32))
        iden = es.enter_context(nc.sbuf_tensor("iden_s", [128, 128], F32))
        ones1 = es.enter_context(nc.sbuf_tensor("ones1", [1, 128], F32))
        epsc = es.enter_context(nc.sbuf_tensor("epsc", [128, 1], F32))
        tcol = es.enter_context(nc.sbuf_tensor("tcol", [128, 4], F32))
        bnbuf = es.enter_context(nc.sbuf_tensor("bnbuf", [128, 16], F32))
        hbuf1 = es.enter_context(nc.sbuf_tensor("hbuf1", [128, DM], F32))
        PS = [es.enter_context(nc.psum_tensor(f"ps{i}", [128, 512], F32)) for i in range(8)]
        seq = es.enter_context(nc.semaphore("seq"))
        block = es.enter_context(nc.Block())

        sems = {"seq": [seq, 0]}
        ENG = {"tensor": "pe", "vector": "ve", "scalar": "ae", "gpsimd": "ge", "sync": "sy"}
        waited = {}

        def sem_of(name):
            if name not in sems:
                sems[name] = [es.enter_context(nc.semaphore(name)), 0]
            return sems[name]

        for _e in ENG.values():
            sem_of(_e)

        def stage(eng, fn, inc=1, auto=True, deps=None, sem=None):
            sname = sem or ENG[eng]
            s = sem_of(sname)
            dl = [(n, x[1]) for n, x in sems.items()] if deps is None else list(deps)
            own = ENG[eng]
            dl = list(dl) + [(own, sems[own][1])]
            w = waited.setdefault(eng, {})
            final = []
            for d in dl:
                if d is None:
                    continue
                n, v = d
                if v <= 0 or w.get(n, 0) >= v:
                    continue
                final.append((n, v))
                w[n] = v

            def _f(e, fn=fn, final=final, inc=inc, auto=auto, s=s):
                for n, v in final:
                    e.wait_ge(sems[n][0], v)
                last = fn(e)
                if auto:
                    last.then_inc(s[0], inc)
            getattr(block, eng)(_f)
            if auto:
                s[1] += inc
            return (sname, s[1])

        def dmas(pairs, sem="lmisc", deps=None):
            s = sem_of(sem)

            def _f(e, pairs=pairs, s=s):
                for dst, src in pairs:
                    e.dma_start(out=dst, in_=src).then_inc(s[0], 16)
            stage("sync", _f, inc=0, auto=False, deps=deps)
            s[1] += 16 * len(pairs)
            return (sem, s[1])

        def Xt(i):
            return X[:, DM * i:DM * (i + 1)]

        def emit_ln(src, hdst, save_i=None, deps=None):
            src = src[:, 0:DM]
            hdst = hdst[:, 0:DM]

            def f_bn(v):
                v.bn_stats(bnbuf[:, 0:6], src[:, 0:512])
                v.bn_stats(bnbuf[:, 6:12], src[:, 512:1024])
                v.drain()
                return v.bn_aggr(bnbuf[:, 12:14], bnbuf[:, 0:12])
            t_bn = stage("vector", f_bn, deps=deps)
            t_sq = stage("scalar", lambda s: s.activation(bnbuf[:, 14:15], bnbuf[:, 13:14], AF.Sqrt,
                                                          bias=epsc[:], scale=1.0), deps=[t_bn])

            def f_rs(v):
                v.reciprocal(tcol[:, 2:3], bnbuf[:, 14:15])
                v.tensor_copy(tcol[:, 1:2], bnbuf[:, 12:13])
                v.drain()
                return v.tensor_scalar(tcol[:, 0:1], tcol[:, 1:2], tcol[:, 2:3], -1.0,
                                       op0=OP.mult, op1=OP.mult)
            t_rs = stage("vector", f_rs, deps=[t_sq])
            if save_i is not None:
                stage("vector", lambda v: v.tensor_copy(MRS[:, 4 * save_i:4 * save_i + 2], tcol[:, 1:3]),
                      deps=[t_rs])
            return stage("scalar", lambda s: s.activation(hdst, src, AF.Identity,
                                                          scale=tcol[:, 2:3], bias=tcol[:, 0:1]),
                         deps=[t_rs])

        last_tsp = [None, None]

        def emit_transpose_split(srcf, dst_hi, dst_lo, col_of, deps=None):
            def f_tr(t):
                last = None
                for c in range(CC):
                    bank = PS[6 + c // 4]
                    last = t.transpose(bank[:, 128 * (c % 4):128 * (c % 4) + 128],
                                       srcf[:, 128 * c:128 * (c + 1)], iden[:])
                return last
            dtr = None if deps is None else (list(deps) + [t for t in last_tsp if t])
            t_tr = stage("tensor", f_tr, deps=dtr)

            def f_hi(s):
                last = None
                for c in range(CC):
                    bank = PS[6 + c // 4]
                    last = s.copy(dst_hi[:, col_of(c):col_of(c) + 128],
                                  bank[:, 128 * (c % 4):128 * (c % 4) + 128])
                return last
            t_hi = stage("scalar", f_hi, deps=[t_tr])

            def f_lo(v):
                last = None
                for c in range(CC):
                    bank = PS[6 + c // 4]
                    last = v.tensor_tensor(dst_lo[:, col_of(c):col_of(c) + 128],
                                           bank[:, 128 * (c % 4):128 * (c % 4) + 128],
                                           dst_hi[:, col_of(c):col_of(c) + 128], op=OP.subtract)
                return last
            t_lo = stage("vector", f_lo, deps=[t_hi])
            last_tsp[0], last_tsp[1] = t_hi, t_lo
            return (t_tr, t_hi, t_lo)

        def emit_mm3(t, pslice, lhsT_hi, lhsT_lo, rhs_hi, rhs_lo, nk=CC):
            last = None
            for c in range(nk):
                last = t.matmul(pslice, lhsT_hi(c), rhs_hi(c), start=(c == 0), stop=False)
                last = t.matmul(pslice, lhsT_hi(c), rhs_lo(c), start=False, stop=False)
                last = t.matmul(pslice, lhsT_lo(c), rhs_hi(c), start=False, stop=(c == nk - 1))
            return last

        def emit_bcast_row(src_row, dst, rowbuf):
            dmas([(rowbuf[:], src_row)])

            def f_mm(t):
                t.matmul(PS[0][:, 0:512], ones1[:], rowbuf[:, 0:512], start=True, stop=True)
                return t.matmul(PS[1][:, 0:512], ones1[:], rowbuf[:, 512:1024], start=True, stop=True)
            stage("tensor", f_mm)

            def f_ev(s):
                s.copy(dst[:, 0:512], PS[0][:, 0:512])
                return s.copy(dst[:, 512:1024], PS[1][:, 0:512])
            stage("scalar", f_ev)

        def emit_load_split_w(src, whi, wlo, wldf):
            for c in range(CC):
                dmas([(wldf[:], src[128 * c:128 * (c + 1), :])])

                def f(v, c=c):
                    v.tensor_copy(whi[:, DM * c:DM * (c + 1)], wldf[:])
                    v.drain()
                    return v.tensor_tensor(wlo[:, DM * c:DM * (c + 1)], wldf[:],
                                           whi[:, DM * c:DM * (c + 1)], op=OP.subtract)
                stage("vector", f)

        with contextlib.ExitStack() as p0:
            ibuf = p0.enter_context(nc.sbuf_tensor("p0_ibuf", [128, HID], I32))
            bbuf = p0.enter_context(nc.sbuf_tensor("p0_bbuf", [128, HID], BF16))
            wf = p0.enter_context(nc.sbuf_tensor("p0_wf", [128, DI2], F32))
            sbh = p0.enter_context(nc.sbuf_tensor("p0_sbh", [128, DI2], BF16))
            sbl = p0.enter_context(nc.sbuf_tensor("p0_sbl", [128, DI2], BF16))

            dmas([(Xt(i), A["x"][128 * i:128 * (i + 1), :]) for i in range(TT)])
            dmas([(iden[:], A["iden"])])
            stage("vector", lambda v: v.memset(ones1[:], 1.0))
            stage("vector", lambda v: v.memset(epsc[:], 1e-5))

            for src, dst, rows, width in ((A["wgT"], st_wg, CC, HID), (A["wuT"], st_wu, CC, HID),
                                          (A["wdT"], st_wd, HC, DM)):
                for j in range(rows):
                    dmas([(ibuf[:, 0:width], src[128 * j:128 * (j + 1), :])])
                    stage("vector", lambda v, width=width: v.tensor_copy(bbuf[:, 0:width], ibuf[:, 0:width]))
                    dmas([(dst[128 * j:128 * (j + 1), :], bbuf[:, 0:width])])

            for j in range(CC):
                dmas([(wf[:], A["winT"][128 * j:128 * (j + 1), :])])

                def f_sp(v):
                    v.tensor_copy(sbh[:], wf[:])
                    v.drain()
                    return v.tensor_tensor(sbl[:], wf[:], sbh[:], op=OP.subtract)
                stage("vector", f_sp)
                dmas([(st_wih[o, :, j, :], sbh[:, 128 * o:128 * (o + 1)]) for o in range(OC)]
                     + [(st_wil[o, :, j, :], sbl[:, 128 * o:128 * (o + 1)]) for o in range(OC)])

            for j in range(DC):
                dmas([(wf[:, 0:DM], A["woutT"][128 * j:128 * (j + 1), :])])

                def f_sp2(v):
                    v.tensor_copy(sbh[:, 0:DM], wf[:, 0:DM])
                    v.drain()
                    return v.tensor_tensor(sbl[:, 0:DM], wf[:, 0:DM], sbh[:, 0:DM], op=OP.subtract)
                stage("vector", f_sp2)
                dmas([(st_woh[128 * j:128 * (j + 1), :], sbh[:, 0:DM]),
                      (st_wol[128 * j:128 * (j + 1), :], sbl[:, 0:DM])])

        with contextlib.ExitStack() as pat:
            hbuf2 = pat.enter_context(nc.sbuf_tensor("hbuf2", [128, DM], F32))
            tsp_hi = pat.enter_context(nc.sbuf_tensor("tsp_hi", [128, DM], BF16))
            tsp_lo = pat.enter_context(nc.sbuf_tensor("tsp_lo", [128, DM], BF16))
            rowbuf = pat.enter_context(nc.sbuf_tensor("rowbuf", [1, DM], F32))
            normw_b = pat.enter_context(nc.sbuf_tensor("normw_b", [128, DM], F32))
            normb_b = pat.enter_context(nc.sbuf_tensor("normb_b", [128, DM], F32))

            emit_bcast_row(A["normw"], normw_b, rowbuf)
            emit_bcast_row(A["normb"], normb_b, rowbuf)

            def emit_h1(i, save):
                if save:
                    emit_ln(Xt(i), hbuf1, save_i=i)
                    emit_ln(hbuf1, hbuf2)
                    stage("vector", lambda v: v.tensor_copy(MRS[:, 4 * i + 2:4 * i + 4], tcol[:, 1:3]))
                else:
                    def f_b0(v):
                        return v.tensor_scalar(tcol[:, 0:1], MRS[:, 4 * i:4 * i + 1],
                                               MRS[:, 4 * i + 1:4 * i + 2], -1.0, op0=OP.mult, op1=OP.mult)
                    stage("vector", f_b0)
                    stage("scalar", lambda s: s.activation(hbuf1[:], Xt(i), AF.Identity,
                                                           scale=MRS[:, 4 * i + 1:4 * i + 2], bias=tcol[:, 0:1]))

                    def f_b2(v):
                        return v.tensor_scalar(tcol[:, 3:4], MRS[:, 4 * i + 2:4 * i + 3],
                                               MRS[:, 4 * i + 3:4 * i + 4], -1.0, op0=OP.mult, op1=OP.mult)
                    stage("vector", f_b2)
                    stage("scalar", lambda s: s.activation(hbuf2[:], hbuf1[:], AF.Identity,
                                                           scale=MRS[:, 4 * i + 3:4 * i + 4], bias=tcol[:, 3:4]))

                def f_aff(v):
                    v.tensor_tensor(hbuf1[:], hbuf2[:], normw_b[:], op=OP.mult)
                    v.drain()
                    return v.tensor_tensor(hbuf1[:], hbuf1[:], normb_b[:], op=OP.add)
                stage("vector", f_aff)

            with contextlib.ExitStack() as pa:
                bk_b = pa.enter_context(nc.sbuf_tensor("bk_b", [128, DM], F32))
                bv_b = pa.enter_context(nc.sbuf_tensor("bv_b", [128, DM], F32))
                wk_hi = pa.enter_context(nc.sbuf_tensor("wk_hi", [128, CC * DM], BF16))
                wk_lo = pa.enter_context(nc.sbuf_tensor("wk_lo", [128, CC * DM], BF16))
                wv_hi = pa.enter_context(nc.sbuf_tensor("wv_hi", [128, CC * DM], BF16))
                wv_lo = pa.enter_context(nc.sbuf_tensor("wv_lo", [128, CC * DM], BF16))
                wldf = pa.enter_context(nc.sbuf_tensor("wldf_a", [128, DM], F32))
                kv_hi = pa.enter_context(nc.sbuf_tensor("kv_hi", [128, DM], BF16))
                kv_lo = pa.enter_context(nc.sbuf_tensor("kv_lo", [128, DM], BF16))

                emit_bcast_row(A["bk"], bk_b, rowbuf)
                emit_bcast_row(A["bv"], bv_b, rowbuf)
                emit_load_split_w(A["wkT"], wk_hi, wk_lo, wldf)
                emit_load_split_w(A["wvT"], wv_hi, wv_lo, wldf)

                for i in range(TT):
                    emit_h1(i, save=True)
                    emit_transpose_split(hbuf1, tsp_hi, tsp_lo, lambda c: 128 * c)
                    for whi, wlo, bias_b, sth, stl in (
                            (wk_hi, wk_lo, bk_b, st_kh, st_kl),
                            (wv_hi, wv_lo, bv_b, st_vh, st_vl)):
                        for half in range(2):
                            def f_mm(t, whi=whi, wlo=wlo, half=half):
                                return emit_mm3(
                                    t, PS[half][:, 0:512],
                                    lambda c: tsp_hi[:, 128 * c:128 * (c + 1)],
                                    lambda c: tsp_lo[:, 128 * c:128 * (c + 1)],
                                    lambda c: whi[:, DM * c + 512 * half:DM * c + 512 * (half + 1)],
                                    lambda c: wlo[:, DM * c + 512 * half:DM * c + 512 * (half + 1)])
                            stage("tensor", f_mm)

                            def f_ev(v, half=half, bias_b=bias_b):
                                return v.tensor_tensor(hbuf2[:, 512 * half:512 * (half + 1)],
                                                       PS[half][:, 0:512],
                                                       bias_b[:, 512 * half:512 * (half + 1)], op=OP.add)
                            stage("vector", f_ev)

                        def f_sp(v):
                            v.tensor_copy(kv_hi[:], hbuf2[:])
                            v.drain()
                            return v.tensor_tensor(kv_lo[:], hbuf2[:], kv_hi[:], op=OP.subtract)
                        stage("vector", f_sp)
                        dmas([(sth[128 * i:128 * (i + 1), :], kv_hi[:]),
                              (stl[128 * i:128 * (i + 1), :], kv_lo[:])])

            with contextlib.ExitStack() as pa:
                S_hi = pa.enter_context(nc.sbuf_tensor("S_hi", [128, CC * DM], BF16))
                S_lo = pa.enter_context(nc.sbuf_tensor("S_lo", [128, CC * DM], BF16))
                with contextlib.ExitStack() as ps_:
                    kt_hi = ps_.enter_context(nc.sbuf_tensor("kt_hi", [128, DM], BF16))
                    kt_lo = ps_.enter_context(nc.sbuf_tensor("kt_lo", [128, DM], BF16))
                    vt_hi = ps_.enter_context(nc.sbuf_tensor("vt_hi", [128, DM], BF16))
                    vt_lo = ps_.enter_context(nc.sbuf_tensor("vt_lo", [128, DM], BF16))
                    for r in range(2):
                        for i in range(TT):
                            dmas([(kt_hi[:], st_kh[128 * i:128 * (i + 1), :]),
                                  (kt_lo[:], st_kl[128 * i:128 * (i + 1), :]),
                                  (vt_hi[:], st_vh[128 * i:128 * (i + 1), :]),
                                  (vt_lo[:], st_vl[128 * i:128 * (i + 1), :])])

                            def f_mm(t, i=i, r=r):
                                last = None
                                for c1 in range(CC):
                                    p = PS[c1][:, 0:512]
                                    last = t.matmul(p, kt_hi[:, 128 * c1:128 * (c1 + 1)],
                                                    vt_hi[:, 512 * r:512 * (r + 1)],
                                                    start=(i == 0), stop=False)
                                    last = t.matmul(p, kt_hi[:, 128 * c1:128 * (c1 + 1)],
                                                    vt_lo[:, 512 * r:512 * (r + 1)], start=False, stop=False)
                                    last = t.matmul(p, kt_lo[:, 128 * c1:128 * (c1 + 1)],
                                                    vt_hi[:, 512 * r:512 * (r + 1)],
                                                    start=False, stop=(i == TT - 1))
                                return last
                            stage("tensor", f_mm)

                        def f_evh(s, r=r):
                            last = None
                            for c1 in range(CC):
                                last = s.copy(S_hi[:, DM * c1 + 512 * r:DM * c1 + 512 * (r + 1)],
                                              PS[c1][:, 0:512])
                            return last
                        stage("scalar", f_evh)

                        def f_evl(v, r=r):
                            last = None
                            for c1 in range(CC):
                                last = v.tensor_tensor(S_lo[:, DM * c1 + 512 * r:DM * c1 + 512 * (r + 1)],
                                                       PS[c1][:, 0:512],
                                                       S_hi[:, DM * c1 + 512 * r:DM * c1 + 512 * (r + 1)],
                                                       op=OP.subtract)
                            return last
                        stage("vector", f_evl)

                with contextlib.ExitStack() as pb:
                    bq_b = pb.enter_context(nc.sbuf_tensor("bq_b", [128, DM], F32))
                    bp_b = pb.enter_context(nc.sbuf_tensor("bp_b", [128, DM], F32))
                    wq_hi = pb.enter_context(nc.sbuf_tensor("wq_hi", [128, CC * DM], BF16))
                    wq_lo = pb.enter_context(nc.sbuf_tensor("wq_lo", [128, CC * DM], BF16))
                    wp_hi = pb.enter_context(nc.sbuf_tensor("wp_hi", [128, CC * DM], BF16))
                    wp_lo = pb.enter_context(nc.sbuf_tensor("wp_lo", [128, CC * DM], BF16))
                    wldf = pb.enter_context(nc.sbuf_tensor("wldf_b", [128, DM], F32))

                    emit_bcast_row(A["bq"], bq_b, rowbuf)
                    emit_bcast_row(A["bp"], bp_b, rowbuf)
                    emit_load_split_w(A["wqT"], wq_hi, wq_lo, wldf)
                    emit_load_split_w(A["wpT"], wp_hi, wp_lo, wldf)

                    def mm_pair(bank_base, whi, wlo, evac):
                        for half in range(2):
                            def f_mm(t, half=half):
                                return emit_mm3(
                                    t, PS[bank_base + half][:, 0:512],
                                    lambda c: tsp_hi[:, 128 * c:128 * (c + 1)],
                                    lambda c: tsp_lo[:, 128 * c:128 * (c + 1)],
                                    lambda c: whi[:, DM * c + 512 * half:DM * c + 512 * (half + 1)],
                                    lambda c: wlo[:, DM * c + 512 * half:DM * c + 512 * (half + 1)])
                            stage("tensor", f_mm)
                            evac(half)

                    for i in range(TT):
                        emit_h1(i, save=False)
                        emit_transpose_split(hbuf1, tsp_hi, tsp_lo, lambda c: 128 * c)

                        def ev_q(half):
                            def f(v, half=half):
                                return v.tensor_tensor(hbuf2[:, 512 * half:512 * (half + 1)],
                                                       PS[half][:, 0:512],
                                                       bq_b[:, 512 * half:512 * (half + 1)], op=OP.add)
                            stage("vector", f)
                        mm_pair(0, wq_hi, wq_lo, ev_q)
                        emit_transpose_split(hbuf2, tsp_hi, tsp_lo, lambda c: 128 * c)

                        def ev_o1(half):
                            stage("scalar", lambda s, half=half: s.activation(
                                hbuf1[:, 512 * half:512 * (half + 1)], PS[2 + half][:, 0:512],
                                AF.Identity, scale=float(1.0 / 32.0)))
                        for half in range(2):
                            def f_mm(t, half=half):
                                return emit_mm3(
                                    t, PS[2 + half][:, 0:512],
                                    lambda c: tsp_hi[:, 128 * c:128 * (c + 1)],
                                    lambda c: tsp_lo[:, 128 * c:128 * (c + 1)],
                                    lambda c: S_hi[:, DM * c + 512 * half:DM * c + 512 * (half + 1)],
                                    lambda c: S_lo[:, DM * c + 512 * half:DM * c + 512 * (half + 1)])
                            stage("tensor", f_mm)
                            ev_o1(half)
                        emit_transpose_split(hbuf1, tsp_hi, tsp_lo, lambda c: 128 * c)

                        def ev_p(half):
                            def f(v, half=half):
                                return v.tensor_tensor(hbuf2[:, 512 * half:512 * (half + 1)],
                                                       PS[4 + half][:, 0:512],
                                                       bp_b[:, 512 * half:512 * (half + 1)], op=OP.add)
                            stage("vector", f)
                        for half in range(2):
                            def f_mm(t, half=half):
                                return emit_mm3(
                                    t, PS[4 + half][:, 0:512],
                                    lambda c: tsp_hi[:, 128 * c:128 * (c + 1)],
                                    lambda c: tsp_lo[:, 128 * c:128 * (c + 1)],
                                    lambda c: wp_hi[:, DM * c + 512 * half:DM * c + 512 * (half + 1)],
                                    lambda c: wp_lo[:, DM * c + 512 * half:DM * c + 512 * (half + 1)])
                            stage("tensor", f_mm)
                            ev_p(half)

                        def f_b0(v, i=i):
                            return v.tensor_scalar(tcol[:, 0:1], MRS[:, 4 * i:4 * i + 1],
                                                   MRS[:, 4 * i + 1:4 * i + 2], -1.0,
                                                   op0=OP.mult, op1=OP.mult)
                        stage("vector", f_b0)
                        stage("scalar", lambda s, i=i: s.activation(hbuf1[:], Xt(i), AF.Identity,
                                                                    scale=MRS[:, 4 * i + 1:4 * i + 2],
                                                                    bias=tcol[:, 0:1]))

                        def f_res(v, i=i):
                            v.tensor_tensor(hbuf2[:], hbuf2[:], hbuf1[:], op=OP.add)
                            v.drain()
                            return v.tensor_tensor(Xt(i), Xt(i), hbuf2[:], op=OP.add)
                        stage("vector", f_res)

            if debug:
                dmas([(dbg["dbg_attn"][128 * i:128 * (i + 1), :], Xt(i)) for i in range(TT)])

        def emit_ffn(tag):
            with contextlib.ExitStack() as pf:
                xlnT_hi = pf.enter_context(nc.sbuf_tensor(f"xlnT_hi_{tag}", [128, CC * T], BF16))
                xlnT_lo = pf.enter_context(nc.sbuf_tensor(f"xlnT_lo_{tag}", [128, CC * T], BF16))
                wg_q = pf.enter_context(nc.sbuf_tensor(f"wg_q_{tag}", [128, CC * 768], BF16))
                wu_q = pf.enter_context(nc.sbuf_tensor(f"wu_q_{tag}", [128, CC * 768], BF16))
                wd_q = pf.enter_context(nc.sbuf_tensor(f"wd_q_{tag}", [128, 6 * DM], BF16))
                sg = pf.enter_context(nc.sbuf_tensor(f"sg_{tag}", [128, 256], F32))
                hhf = pf.enter_context(nc.sbuf_tensor(f"hhf_{tag}", [128, 256], F32))
                hh_hi = pf.enter_context(nc.sbuf_tensor(f"hh_hi_{tag}", [128, 6 * 256], BF16))
                hh_lo = pf.enter_context(nc.sbuf_tensor(f"hh_lo_{tag}", [128, 6 * 256], BF16))

                for i in range(TT):
                    emit_ln(Xt(i), hbuf1)
                    emit_transpose_split(hbuf1, xlnT_hi, xlnT_lo, lambda c, i=i: T * c + 128 * i)

                for q in range(4):
                    dmas([(wg_q[:].rearrange("p (c h) -> p c h", h=768),
                           st_wg.rearrange("(c p) h -> p c h", p=128)[:, :, 768 * q:768 * (q + 1)]),
                          (wu_q[:].rearrange("p (c h) -> p c h", h=768),
                           st_wu.rearrange("(c p) h -> p c h", p=128)[:, :, 768 * q:768 * (q + 1)])])
                    dmas([(wd_q[:, DM * h:DM * (h + 1)],
                           st_wd[128 * (6 * q + h):128 * (6 * q + h + 1), :]) for h in range(6)])

                    for sl in range(NSC):
                        t0 = 256 * sl
                        for h in range(6):
                            def f_g(t, h=h, t0=t0):
                                last = None
                                for wq_, bank in ((wg_q, PS[0]), (wu_q, PS[1])):
                                    for c in range(CC):
                                        last = t.matmul(bank[:, 0:256],
                                                        wq_[:, 768 * c + 128 * h:768 * c + 128 * (h + 1)],
                                                        xlnT_hi[:, T * c + t0:T * c + t0 + 256],
                                                        start=(c == 0), stop=False)
                                        last = t.matmul(bank[:, 0:256],
                                                        wq_[:, 768 * c + 128 * h:768 * c + 128 * (h + 1)],
                                                        xlnT_lo[:, T * c + t0:T * c + t0 + 256],
                                                        start=False, stop=(c == CC - 1))
                                return last
                            stage("tensor", f_g)
                            stage("scalar", lambda s: s.activation(sg[:], PS[0][:, 0:256],
                                                                   AF.Silu, scale=S127))

                            def f_hh(v, h=h):
                                v.tensor_tensor(hhf[:], sg[:], PS[1][:, 0:256], op=OP.mult)
                                v.drain()
                                v.tensor_copy(hh_hi[:, 256 * h:256 * (h + 1)], hhf[:])
                                v.drain()
                                return v.tensor_tensor(hh_lo[:, 256 * h:256 * (h + 1)], hhf[:],
                                                       hh_hi[:, 256 * h:256 * (h + 1)], op=OP.subtract)
                            stage("vector", f_hh)

                            def f_d(t, h=h):
                                last = None
                                for tsub in range(2):
                                    for ch in range(2):
                                        bank = PS[2 + 2 * tsub + ch]
                                        last = t.matmul(
                                            bank[:, 0:512],
                                            hh_hi[:, 256 * h + 128 * tsub:256 * h + 128 * (tsub + 1)],
                                            wd_q[:, DM * h + 512 * ch:DM * h + 512 * (ch + 1)],
                                            start=(h == 0), stop=False)
                                        last = t.matmul(
                                            bank[:, 0:512],
                                            hh_lo[:, 256 * h + 128 * tsub:256 * h + 128 * (tsub + 1)],
                                            wd_q[:, DM * h + 512 * ch:DM * h + 512 * (ch + 1)],
                                            start=False, stop=(h == 5))
                                return last
                            stage("tensor", f_d)

                        def f_acc(v, sl=sl):
                            last = None
                            for tsub in range(2):
                                i = 2 * sl + tsub
                                for ch in range(2):
                                    bank = PS[2 + 2 * tsub + ch]
                                    xs = X[:, DM * i + 512 * ch:DM * i + 512 * (ch + 1)]
                                    last = v.scalar_tensor_tensor(xs, bank[:, 0:512], S127SQ, xs,
                                                                  op0=OP.mult, op1=OP.add)
                            return last
                        stage("vector", f_acc)

                if debug and tag == "f1":
                    dmas([(dbg["dbg_ffn1"][128 * i:128 * (i + 1), :], Xt(i)) for i in range(TT)])

        emit_ffn("f1")

        with contextlib.ExitStack() as pm:
            A_r = pm.enter_context(nc.sbuf_tensor("A_r", [128, DC * NST], F32))
            D_r = pm.enter_context(nc.sbuf_tensor("D_r", [128, DC], F32))
            bconv_r = pm.enter_context(nc.sbuf_tensor("bconv_r", [128, DC], F32))
            bdt_r = pm.enter_context(nc.sbuf_tensor("bdt_r", [128, DC], F32))
            wconv_r = pm.enter_context(nc.sbuf_tensor("wconv_r", [128, DC * 4], F32))
            wxr_s = pm.enter_context(nc.sbuf_tensor("wxr_s", [128, DC * 96], F32))
            sel_s = pm.enter_context(nc.sbuf_tensor("sel_s", [NST, NST * 128], F32))
            xcb = pm.enter_context(nc.sbuf_tensor("xcb", [128, DC * 259], F32))
            ub = pm.enter_context(nc.sbuf_tensor("ub", [128, DC * 256], F32))
            deltab = pm.enter_context(nc.sbuf_tensor("deltab", [128, DC * 256], F32))
            gb = pm.enter_context(nc.sbuf_tensor("gb", [128, DC * 256], F32))
            sres = pm.enter_context(nc.sbuf_tensor("sres", [128, DC * 256], F32))
            halob = pm.enter_context(nc.sbuf_tensor("halob", [128, DC * 3], F32))
            scan_init = pm.enter_context(nc.sbuf_tensor("scan_init", [128, DC * NST], F32))
            Bb = pm.enter_context(nc.sbuf_tensor("Bb", [128, 2 * 256], F32))
            Cb = pm.enter_context(nc.sbuf_tensor("Cb", [128, 2 * 256], F32))
            dt_sb = pm.enter_context(nc.sbuf_tensor("dt_sb", [DTR, 256], F32))
            B_sb = pm.enter_context(nc.sbuf_tensor("B_sb", [NST, 256], F32))
            C_sb = pm.enter_context(nc.sbuf_tensor("C_sb", [NST, 256], F32))
            wob_hi = pm.enter_context(nc.sbuf_tensor("wob_hi", [128, 2 * DM], BF16))
            wob_lo = pm.enter_context(nc.sbuf_tensor("wob_lo", [128, 2 * DM], BF16))

            dmas([(A_r[:], A["alog"]), (D_r[:], A["dvec"]), (bconv_r[:], A["bconv"]),
                  (bdt_r[:], A["bdt"]), (wconv_r[:], A["wconv"]), (wxr_s[:], A["wxr"]),
                  (sel_s[:], A["sel"])])
            stage("scalar", lambda s: s.activation(A_r[:], A_r[:], AF.Exp, scale=1.0))
            stage("vector", lambda v: v.tensor_scalar(A_r[:], A_r[:], -1.0, None, op0=OP.mult))
            stage("vector", lambda v: v.memset(halob[:], 0.0))
            stage("vector", lambda v: v.memset(scan_init[:], 0.0))

            xcb3 = xcb[:].rearrange("p (c k) -> p c k", k=259)
            dA = xcb[:, 0:DC * 129]
            dBu = xcb[:, DC * 129:DC * 258]
            dA3 = dA.rearrange("p (c k) -> p c k", k=129)
            dBu3 = dBu.rearrange("p (c k) -> p c k", k=129)
            si3 = scan_init[:].rearrange("p (c n) -> p c n", n=NST)
            gb3 = gb[:].rearrange("p (c k) -> p c k", k=256)
            ub3 = ub[:].rearrange("p (c k) -> p c k", k=256)

            fresh = [None]

            def mstage(eng, fn, deps, **kw):
                if eng not in fresh[0]:
                    fresh[0].add(eng)
                    return stage(eng, fn, deps=None, **kw)
                return stage(eng, fn, deps=[d for d in deps if d], **kw)

            def mdmas(pairs, sem, deps):
                if "sync" not in fresh[0]:
                    fresh[0].add("sync")
                    return dmas(pairs, sem=sem, deps=None)
                return dmas(pairs, sem=sem, deps=[d for d in deps if d])

            for sc in range(NSC):
                i0 = 2 * sc
                fresh[0] = set()
                bfree = {b: [] for b in range(8)}

                pfr = contextlib.ExitStack()
                xlnT_hi = pfr.enter_context(nc.sbuf_tensor(f"xlnT_hi_m{sc}", [128, CC * 256], BF16))
                xlnT_lo = pfr.enter_context(nc.sbuf_tensor(f"xlnT_lo_m{sc}", [128, CC * 256], BF16))
                wib_hi = pfr.enter_context(nc.sbuf_tensor(f"wib_hi{sc}", [128, 2 * CC * 128], BF16))
                wib_lo = pfr.enter_context(nc.sbuf_tensor(f"wib_lo{sc}", [128, 2 * CC * 128], BF16))
                wdtT_s = pfr.enter_context(nc.sbuf_tensor(f"wdtT_s{sc}", [DTR, DI], F32))
                z1 = pfr.enter_context(nc.sbuf_tensor(f"z1_{sc}", [128, 2 * 256], F32))
                t_wdt = mdmas([(wdtT_s[:], A["wdtT"])], sem="lmisc", deps=[None])

                t_lo_all = []
                for tl in range(2):
                    dep0 = None if tl == 0 else [t_lo_all[-1]]
                    t_ap = emit_ln(Xt(i0 + tl), hbuf1, deps=dep0)
                    fresh[0].update(("vector", "scalar"))
                    ttr, thi, tlo = emit_transpose_split(hbuf1, xlnT_hi, xlnT_lo,
                                                         lambda c, tl=tl: 256 * c + 128 * tl,
                                                         deps=[t_ap])
                    fresh[0].add("tensor")
                    t_lo_all.append(tlo)
                bfree[6] = [thi, tlo]
                bfree[7] = [thi, tlo]
                t_xln = [thi, tlo]

                t_wimm = [None] * OC
                t_wevac = [None] * OC
                t_wld = [None] * OC
                for o in range(OC):
                    sl = o % 2
                    t_wld[o] = mdmas(
                        [(wib_hi[:, sl * DM:sl * DM + DM].rearrange("p (c k) -> p c k", k=128), st_wih[o]),
                         (wib_lo[:, sl * DM:sl * DM + DM].rearrange("p (c k) -> p c k", k=128), st_wil[o])],
                        sem=f"lwi{sl}", deps=[t_wimm[o - 2] if o >= 2 else None])
                    bank = o % 4

                    def f_mm(t, o=o, sl=sl, bank=bank):
                        return emit_mm3(
                            t, PS[bank][:, 0:256],
                            lambda c: wib_hi[:, sl * DM + 128 * c:sl * DM + 128 * (c + 1)],
                            lambda c: wib_lo[:, sl * DM + 128 * c:sl * DM + 128 * (c + 1)],
                            lambda c: xlnT_hi[:, 256 * c:256 * (c + 1)],
                            lambda c: xlnT_lo[:, 256 * c:256 * (c + 1)])
                    t_wimm[o] = mstage("tensor", f_mm,
                                       deps=[t_wld[o]] + t_xln + bfree[bank])
                    if o < DC:
                        t_wevac[o] = mstage("scalar", lambda s, o=o, bank=bank: s.copy(
                            xcb[:, 259 * o + 3:259 * (o + 1)], PS[bank][:, 0:256]),
                            deps=[t_wimm[o]])
                    else:
                        t_wevac[o] = mstage("scalar", lambda s, o=o, bank=bank: s.activation(
                            sres[:, 256 * (o - DC):256 * (o - DC + 1)], PS[bank][:, 0:256], AF.Silu),
                            deps=[t_wimm[o]])
                    bfree[bank] = [t_wevac[o]]

                t_hin = mstage("vector", lambda v: v.tensor_copy(
                    xcb3[:, :, 0:3], halob[:].rearrange("p (c k) -> p c k", k=3)),
                    deps=[t_wevac[DC - 1]])
                t_cv = None
                for j in range(4):
                    def f_cv(v, j=j):
                        last = None
                        for c in range(DC):
                            acc = ub[:, 256 * c:256 * (c + 1)]
                            xs = xcb[:, 259 * c + j:259 * c + j + 256]
                            wj = wconv_r[:, 4 * c + j:4 * c + j + 1]
                            if j == 0:
                                last = v.tensor_scalar(acc, xs, wj, None, op0=OP.mult)
                            else:
                                last = v.scalar_tensor_tensor(acc, xs, wj, acc, op0=OP.mult, op1=OP.add)
                        return last
                    t_cv = mstage("vector", f_cv,
                                  deps=[t_hin, t_wevac[DC - 1], ("ve", sems["ve"][1])])
                t_hsv = mstage("vector", lambda v: v.tensor_copy(
                    halob[:].rearrange("p (c k) -> p c k", k=3), xcb3[:, :, 256:259]),
                    deps=[t_cv])

                def f_u(s):
                    last = None
                    for c in range(DC):
                        last = s.activation(ub[:, 256 * c:256 * (c + 1)], ub[:, 256 * c:256 * (c + 1)],
                                            AF.Silu, bias=bconv_r[:, c:c + 1], scale=1.0)
                    return last
                t_u = mstage("scalar", f_u, deps=[t_cv])

                def f_xp(t):
                    last = None
                    for lo, hi, bank, rows in ((0, 64, PS[4], DTR), (64, 80, PS[5], NST), (80, 96, PS[6], NST)):
                        for c in range(DC):
                            last = t.matmul(bank[0:rows, 0:256], wxr_s[:, 96 * c + lo:96 * c + hi],
                                            ub[:, 256 * c:256 * (c + 1)],
                                            start=(c == 0), stop=(c == DC - 1))
                    return last
                t_xp = mstage("tensor", f_xp, deps=[t_u] + bfree[4] + bfree[5] + bfree[6])

                def f_xpe(s):
                    s.copy(dt_sb[:], PS[4][0:DTR, 0:256])
                    s.copy(B_sb[:], PS[5][0:NST, 0:256])
                    return s.copy(C_sb[:], PS[6][0:NST, 0:256])
                t_xpe = mstage("scalar", f_xpe, deps=[t_xp])
                bfree[4] = bfree[5] = bfree[6] = [t_xpe]

                t_dmm = [None] * DC
                t_fm = [None] * DC
                t_abs = [None] * DC
                t_fadd = [None] * DC
                for c in range(DC):
                    bank = c % 4
                    t_dmm[c] = mstage("tensor", lambda t, c=c, bank=bank: t.matmul(
                        PS[bank][:, 0:256], wdtT_s[:, 128 * c:128 * (c + 1)], dt_sb[:],
                        start=True, stop=True),
                        deps=[t_xpe, t_wdt] + bfree[bank])
                    t_fm[c] = mstage("vector", lambda v, c=c, bank=bank: v.tensor_scalar(
                        deltab[:, 256 * c:256 * (c + 1)], PS[bank][:, 0:256],
                        bdt_r[:, c:c + 1], 0.0, op0=OP.add, op1=OP.max),
                        deps=[t_dmm[c]])
                    zsl = z1[:, 256 * (c % 2):256 * (c % 2) + 256]
                    t_abs[c] = mstage("scalar", lambda s, c=c, bank=bank, zsl=zsl: s.activation(
                        zsl, PS[bank][:, 0:256], AF.Abs, bias=bdt_r[:, c:c + 1], scale=1.0),
                        deps=[t_dmm[c], t_fadd[c - 2] if c >= 2 else None])
                    bfree[bank] = [t_fm[c], t_abs[c]]
                    t_e = mstage("scalar", lambda s, zsl=zsl: s.activation(zsl, zsl, AF.Exp, scale=-1.0),
                                 deps=[t_abs[c]])
                    t_l = mstage("scalar", lambda s, zsl=zsl: s.activation(zsl, zsl, AF.Ln, bias=1.0, scale=1.0),
                                 deps=[t_e])
                    t_fadd[c] = mstage("vector", lambda v, c=c, zsl=zsl: v.tensor_tensor(
                        deltab[:, 256 * c:256 * (c + 1)], deltab[:, 256 * c:256 * (c + 1)], zsl, op=OP.add),
                        deps=[t_l])

                t_g = mstage("vector", lambda v: v.tensor_tensor(gb[:], deltab[:], ub[:], op=OP.mult),
                             deps=[t_u, t_fadd[DC - 1]])

                def f_ud(v):
                    last = None
                    for c in range(DC):
                        last = v.tensor_scalar(ub[:, 256 * c:256 * (c + 1)], ub[:, 256 * c:256 * (c + 1)],
                                               D_r[:, c:c + 1], None, op0=OP.mult)
                    return last
                t_ud = mstage("vector", f_ud, deps=[t_xp, t_g])
                pfr.close()

                pn = contextlib.ExitStack()
                tmpb = pn.enter_context(nc.sbuf_tensor(f"tmpb_{sc}", [128, DC * 128], F32))
                sl1 = pn.enter_context(nc.sbuf_tensor(f"sl1_{sc}", [128, 2 * DC * 129], F32))
                tmpb3 = tmpb[:].rearrange("p (c k) -> p c k", k=128)
                A3 = A_r[:].rearrange("p (c n) -> p c n", n=NST)
                dA_s = [dA, sl1[:, 0:DC * 129]]
                dBu_s = [dBu, sl1[:, DC * 129:2 * DC * 129]]
                dA3_s = [d.rearrange("p (c k) -> p c k", k=129) for d in dA_s]
                dBu3_s = [d.rearrange("p (c k) -> p c k", k=129) for d in dBu_s]
                t_ms = mstage("vector", lambda v: v.memset(dA3_s[0][:, :, 0:1], 0.0), deps=[t_hsv])
                t_ms = mstage("vector", lambda v: v.memset(dA3_s[1][:, :, 0:1], 0.0), deps=[t_ud])

                U = 2 * NST
                t_dam = [None] * U
                t_dae = [None] * U
                t_dbu = [None] * U
                t_scn = [None] * U
                t_ym = [None] * U
                t_ya = [None] * U
                t_bce = [None] * NST
                t_bcastk = [None, None]
                t_bcek = [None, None]
                for u in range(U + 2):
                    n, sub = u // 2, u % 2
                    slp = u % 2
                    if u < U:
                        if sub == 0:
                            pslot = n % 2
                            pbank = 6 + pslot

                            def f_bc(t, n=n, pbank=pbank):
                                t.matmul(PS[pbank][:, 0:256], sel_s[:, 128 * n:128 * (n + 1)],
                                         B_sb[:], start=True, stop=True)
                                return t.matmul(PS[pbank][:, 256:512], sel_s[:, 128 * n:128 * (n + 1)],
                                                C_sb[:], start=True, stop=True)
                            t_bcastk[pslot] = mstage("tensor", f_bc,
                                                     deps=[t_xpe, t_bcek[pslot]] + bfree[pbank])
                            bfree[pbank] = []

                            def f_bce(s, pslot=pslot, pbank=pbank):
                                s.copy(Bb[:, 256 * pslot:256 * (pslot + 1)], PS[pbank][:, 0:256])
                                return s.copy(Cb[:, 256 * pslot:256 * (pslot + 1)], PS[pbank][:, 256:512])
                            ymdep = [t_ym[2 * (n - 2) + 1] if n >= 2 else None,
                                     t_dbu[2 * (n - 2) + 1] if n >= 2 else None]
                            t_bce[n] = mstage("scalar", f_bce,
                                              deps=[t_bcastk[pslot]] + ymdep)
                            t_bcek[pslot] = t_bce[n]

                        def f_dam(v, n=n, sub=sub, slp=slp):
                            dv = deltab[:].rearrange("p (c k) -> p c k", k=256)[:, :, 128 * sub:128 * (sub + 1)]
                            av = A3[:, :, n:n + 1].broadcast_to([128, DC, 128])
                            return v.tensor_tensor(dA3_s[slp][:, :, 1:129], dv, av, op=OP.mult)
                        t_dam[u] = mstage("vector", f_dam,
                                          deps=[t_scn[u - 2] if u >= 2 else None,
                                                t_fadd[DC - 1], t_ms])

                        t_dae[u] = mstage("scalar", lambda s, slp=slp: s.activation(
                            dA3_s[slp][:, :, 1:129], dA3_s[slp][:, :, 1:129], AF.Exp, scale=1.0),
                            deps=[t_dam[u]])

                        def f_dbu(g_, n=n, sub=sub, slp=slp):
                            gv = gb3[:, :, 128 * sub:128 * (sub + 1)]
                            bv_ = Bb[:, 256 * (n % 2) + 128 * sub:256 * (n % 2) + 128 * (sub + 1)]
                            bv_ = bv_.unsqueeze(1).broadcast_to([128, DC, 128])
                            return g_.tensor_tensor(dBu3_s[slp][:, :, 1:129], gv, bv_, op=OP.mult)
                        t_dbu[u] = mstage("gpsimd", f_dbu,
                                          deps=[t_bce[n], t_g,
                                                t_scn[u - 2] if u >= 2 else None, t_hsv])

                    if 2 <= u:
                        w_ = u - 2
                        if w_ < U:
                            subw = w_ % 2

                            def f_ya(vv, subw=subw):
                                yv = ub3[:, :, 128 * subw:128 * (subw + 1)]
                                return vv.tensor_tensor(yv, yv, tmpb3, op=OP.add)
                            t_ya[w_] = mstage("vector", f_ya, deps=[t_ym[w_], t_ud])

                    if 1 <= u:
                        v = u - 1
                        if v < U:
                            nv, subv, slv = v // 2, v % 2, v % 2

                            def f_scan(vv, nv=nv, slv=slv):
                                vv.tensor_copy(dBu3_s[slv][:, :, 0:1], si3[:, :, nv:nv + 1])
                                vv.drain()
                                vv.tensor_tensor_scan(dBu_s[slv], dA_s[slv], dBu_s[slv],
                                                      initial=0.0, op0=OP.mult, op1=OP.add)
                                vv.drain()
                                return vv.tensor_copy(si3[:, :, nv:nv + 1], dBu3_s[slv][:, :, 128:129])
                            t_scn[v] = mstage("vector", f_scan, deps=[t_dae[v], t_dbu[v]])

                            def f_ym(g_, nv=nv, subv=subv, slv=slv):
                                cv = Cb[:, 256 * (nv % 2) + 128 * subv:256 * (nv % 2) + 128 * (subv + 1)]
                                cv = cv.unsqueeze(1).broadcast_to([128, DC, 128])
                                return g_.tensor_tensor(tmpb3, dBu3_s[slv][:, :, 1:129], cv, op=OP.mult)
                            t_ym[v] = mstage("gpsimd", f_ym,
                                             deps=[t_scn[v], t_ya[v - 1] if v >= 1 else None])

                pn.close()
                t_yadd = {0: t_ya[U - 2], 1: t_ya[U - 1]}
                t_ymul = {0: t_ym[U - 2], 1: t_ym[U - 1]}

                py_ = contextlib.ExitStack()
                ysp_hi = py_.enter_context(nc.sbuf_tensor(f"ysp_hi_{sc}", [128, DC * 256], BF16))
                ysp_lo = py_.enter_context(nc.sbuf_tensor(f"ysp_lo_{sc}", [128, DC * 256], BF16))
                t_yf = mstage("vector", lambda v: v.tensor_tensor(ub[:], ub[:], sres[:], op=OP.mult),
                              deps=[t_yadd[0], t_yadd[1], t_wevac[OC - 1],
                                    t_ymul[0], t_ymul[1]])

                def f_ys(v):
                    v.tensor_copy(ysp_hi[:], ub[:])
                    v.drain()
                    return v.tensor_tensor(ysp_lo[:], ub[:], ysp_hi[:], op=OP.subtract)
                t_ys = mstage("vector", f_ys, deps=[t_yf])

                t_womm = [None] * DC
                t_wold = [None] * DC
                for d in range(DC):
                    sl = d % 2
                    t_wold[d] = mdmas(
                        [(wob_hi[:, sl * DM:sl * DM + DM], st_woh[128 * d:128 * (d + 1), :]),
                         (wob_lo[:, sl * DM:sl * DM + DM], st_wol[128 * d:128 * (d + 1), :])],
                        sem=f"lwo{sl}", deps=[t_womm[d - 2] if d >= 2 else None])

                    def f_wo(t, d=d, sl=sl):
                        last = None
                        for sub in range(2):
                            yh = ysp_hi[:, 256 * d + 128 * sub:256 * d + 128 * (sub + 1)]
                            yl = ysp_lo[:, 256 * d + 128 * sub:256 * d + 128 * (sub + 1)]
                            for ch in range(2):
                                bank = PS[2 * sub + ch]
                                wh = wob_hi[:, sl * DM + 512 * ch:sl * DM + 512 * (ch + 1)]
                                wl = wob_lo[:, sl * DM + 512 * ch:sl * DM + 512 * (ch + 1)]
                                last = t.matmul(bank[:, 0:512], yh, wh, start=(d == 0), stop=False)
                                last = t.matmul(bank[:, 0:512], yh, wl, start=False, stop=False)
                                last = t.matmul(bank[:, 0:512], yl, wh, start=False, stop=(d == DC - 1))
                        return last
                    deps = [t_ys, t_wold[d]]
                    if d == 0:
                        deps += bfree[0] + bfree[1] + bfree[2] + bfree[3]
                    t_womm[d] = mstage("tensor", f_wo, deps=deps)

                def f_xa(v, i0=i0):
                    last = None
                    for sub in range(2):
                        i = i0 + sub
                        for ch in range(2):
                            bank = PS[2 * sub + ch]
                            xs = X[:, DM * i + 512 * ch:DM * i + 512 * (ch + 1)]
                            last = v.scalar_tensor_tensor(xs, bank[:, 0:512], 0.0, xs,
                                                          op0=OP.add, op1=OP.add)
                    return last
                t_xa = mstage("vector", f_xa, deps=[t_womm[DC - 1]])
                for b in range(4):
                    bfree[b] = [t_xa]
                py_.close()

            if debug:
                dmas([(dbg["dbg_mamba"][128 * i:128 * (i + 1), :], Xt(i)) for i in range(TT)])

        emit_ffn("f2")

        dmas([(out[128 * i:128 * (i + 1), :], Xt(i)) for i in range(TT)])
        stage("sync", lambda e: None, inc=0, auto=False)

    return nc


def kernel(**inputs):
    T = 2048
    x = np.ascontiguousarray(inputs["L0_attn_a_x"].astype(np.float32, copy=False))
    w = _prep_weights(inputs)
    nc = build(T=T, debug=False)
    in_maps = []
    for c in range(NCORE):
        m = dict(w)
        m["x"] = np.ascontiguousarray(x[c])
        in_maps.append(m)
    res = run_bass_kernel_spmd(nc, in_maps, core_ids=list(range(NCORE)))
    return np.stack([np.asarray(res.results[c]["out"]) for c in range(NCORE)], axis=0)


# version 07 (final)
# speedup vs baseline: 1.1371x; 1.0639x over previous
import contextlib
import numpy as np

import concourse.bass as bass
import concourse.mybir as mybir
from concourse.alu_op_type import AluOpType as OP
from concourse.bass_utils import run_bass_kernel_spmd

F32 = mybir.dt.float32
BF16 = mybir.dt.bfloat16
I32 = mybir.dt.int32
AF = mybir.ActivationFunctionType

DM = 1024
DI = 2048
DI2 = 4096
NST = 16
DTR = 64
HID = 3072
NCORE = 8
CC = DM // 128
DC = DI // 128
OC = DI2 // 128
HC = HID // 128
S127 = float(np.float32(1.0 / 127.0))
S127SQ = float(np.float32(1.0 / 127.0) * np.float32(1.0 / 127.0))


def _prep_weights(inp):
    c = np.ascontiguousarray
    w = {}
    qkv = inp["L0_attn_a_qkv_w"]
    w["wqT"] = c(qkv[0:DM].T)
    w["wkT"] = c(qkv[DM:2 * DM].T)
    w["wvT"] = c(qkv[2 * DM:3 * DM].T)
    qb = inp["L0_attn_a_qkv_b"]
    w["bq"] = c(qb[None, 0:DM])
    w["bk"] = c(qb[None, DM:2 * DM])
    w["bv"] = c(qb[None, 2 * DM:3 * DM])
    w["wpT"] = c(inp["L0_attn_a_proj_w"].T)
    w["bp"] = c(inp["L0_attn_a_proj_b"][None, :])
    w["normw"] = c(inp["L0_attn_a_norm_w"][None, :])
    w["normb"] = c(inp["L0_attn_a_norm_b"][None, :])
    w["wgT"] = c(inp["L0_ffn_w_gate"].T)
    w["wuT"] = c(inp["L0_ffn_w_up"].T)
    w["wdT"] = c(inp["L0_ffn_w_down"].T)
    w["winT"] = c(inp["L0_attn_b_W_in"].T)
    w["woutT"] = c(inp["L0_attn_b_W_out"].T)
    w["wxr"] = c(inp["L0_attn_b_W_xproj"].T.reshape(DC, 128, 96).transpose(1, 0, 2).reshape(128, DC * 96))
    w["wdtT"] = c(inp["L0_attn_b_W_dt"].T)
    r = lambda v: c(v.reshape(DC, 128).T)
    w["bconv"] = r(inp["L0_attn_b_b_conv"])
    w["bdt"] = r(inp["L0_attn_b_b_dt"])
    w["dvec"] = r(inp["L0_attn_b_D"])
    w["wconv"] = c(inp["L0_attn_b_W_conv"].reshape(DI, 4).reshape(DC, 128, 4).transpose(1, 0, 2).reshape(128, DC * 4))
    w["alog"] = c(inp["L0_attn_b_A_log"].reshape(DC, 128, NST).transpose(1, 0, 2).reshape(128, DC * NST))
    w["iden"] = np.eye(128, dtype=np.float32)
    sel = np.zeros((NST, NST * 128), np.float32)
    for n in range(NST):
        sel[n, 128 * n:128 * (n + 1)] = 1.0
    w["sel"] = sel
    return w


def build(T=2048, debug=False):
    TT = T // 128
    NSC = T // 256
    nc = bass.Bass("TRN2", target_bir_lowering=False)

    dt_in = {
        "x": ([T, DM], F32), "wqT": ([DM, DM], F32), "wkT": ([DM, DM], F32),
        "wvT": ([DM, DM], F32), "bq": ([1, DM], F32), "bk": ([1, DM], F32),
        "bv": ([1, DM], F32), "wpT": ([DM, DM], F32), "bp": ([1, DM], F32),
        "normw": ([1, DM], F32), "normb": ([1, DM], F32),
        "wgT": ([DM, HID], I32), "wuT": ([DM, HID], I32), "wdT": ([HID, DM], I32),
        "winT": ([DM, DI2], F32), "woutT": ([DI, DM], F32),
        "wxr": ([128, DC * 96], F32), "wdtT": ([DTR, DI], F32),
        "bconv": ([128, DC], F32), "bdt": ([128, DC], F32), "dvec": ([128, DC], F32),
        "wconv": ([128, DC * 4], F32), "alog": ([128, DC * NST], F32),
        "iden": ([128, 128], F32), "sel": ([NST, NST * 128], F32),
    }
    A = {k: nc.dram_tensor(k, sh, dt, kind="ExternalInput").ap() for k, (sh, dt) in dt_in.items()}
    out = nc.dram_tensor("out", [T, DM], F32, kind="ExternalOutput").ap()
    dbg = {}
    if debug:
        for k in ("dbg_attn", "dbg_ffn1", "dbg_mamba"):
            dbg[k] = nc.dram_tensor(k, [T, DM], F32, kind="ExternalOutput").ap()

    st_wg = nc.dram_tensor("st_wg", [DM, HID], BF16, kind="Internal").ap()
    st_wu = nc.dram_tensor("st_wu", [DM, HID], BF16, kind="Internal").ap()
    st_wd = nc.dram_tensor("st_wd", [HID, DM], BF16, kind="Internal").ap()
    st_wih = nc.dram_tensor("st_wih", [OC, 128, CC, 128], BF16, kind="Internal").ap()
    st_wil = nc.dram_tensor("st_wil", [OC, 128, CC, 128], BF16, kind="Internal").ap()
    st_woh = nc.dram_tensor("st_woh", [DI, DM], BF16, kind="Internal").ap()
    st_wol = nc.dram_tensor("st_wol", [DI, DM], BF16, kind="Internal").ap()
    st_kh = nc.dram_tensor("st_kh", [T, DM], BF16, kind="Internal").ap()
    st_kl = nc.dram_tensor("st_kl", [T, DM], BF16, kind="Internal").ap()
    st_vh = nc.dram_tensor("st_vh", [T, DM], BF16, kind="Internal").ap()
    st_vl = nc.dram_tensor("st_vl", [T, DM], BF16, kind="Internal").ap()

    es = contextlib.ExitStack()
    with es:
        X = es.enter_context(nc.sbuf_tensor("X", [128, TT * DM], F32))
        MRS = es.enter_context(nc.sbuf_tensor("MRS", [128, TT * 4], F32))
        iden = es.enter_context(nc.sbuf_tensor("iden_s", [128, 128], F32))
        ones1 = es.enter_context(nc.sbuf_tensor("ones1", [1, 128], F32))
        epsc = es.enter_context(nc.sbuf_tensor("epsc", [128, 1], F32))
        tcol = es.enter_context(nc.sbuf_tensor("tcol", [128, 4], F32))
        bnbuf = es.enter_context(nc.sbuf_tensor("bnbuf", [128, 16], F32))
        hbuf1 = es.enter_context(nc.sbuf_tensor("hbuf1", [128, DM], F32))
        PS = [es.enter_context(nc.psum_tensor(f"ps{i}", [128, 512], F32)) for i in range(8)]
        seq = es.enter_context(nc.semaphore("seq"))
        block = es.enter_context(nc.Block())

        sems = {"seq": [seq, 0]}
        ENG = {"tensor": "pe", "vector": "ve", "scalar": "ae", "gpsimd": "ge", "sync": "sy"}
        waited = {}

        def sem_of(name):
            if name not in sems:
                sems[name] = [es.enter_context(nc.semaphore(name)), 0]
            return sems[name]

        for _e in ENG.values():
            sem_of(_e)

        def stage(eng, fn, inc=1, auto=True, deps=None, sem=None):
            sname = sem or ENG[eng]
            s = sem_of(sname)
            dl = [(n, x[1]) for n, x in sems.items()] if deps is None else list(deps)
            own = ENG[eng]
            dl = list(dl) + [(own, sems[own][1])]
            w = waited.setdefault(eng, {})
            final = []
            for d in dl:
                if d is None:
                    continue
                n, v = d
                if v <= 0 or w.get(n, 0) >= v:
                    continue
                final.append((n, v))
                w[n] = v

            def _f(e, fn=fn, final=final, inc=inc, auto=auto, s=s):
                for n, v in final:
                    e.wait_ge(sems[n][0], v)
                last = fn(e)
                if auto:
                    last.then_inc(s[0], inc)
            getattr(block, eng)(_f)
            if auto:
                s[1] += inc
            return (sname, s[1])

        def dmas(pairs, sem="lmisc", deps=None):
            s = sem_of(sem)

            def _f(e, pairs=pairs, s=s):
                for dst, src in pairs:
                    e.dma_start(out=dst, in_=src).then_inc(s[0], 16)
            stage("sync", _f, inc=0, auto=False, deps=deps)
            s[1] += 16 * len(pairs)
            return (sem, s[1])

        def Xt(i):
            return X[:, DM * i:DM * (i + 1)]

        def emit_ln(src, hdst, save_i=None, deps=None):
            src = src[:, 0:DM]
            hdst = hdst[:, 0:DM]

            def f_bn(v):
                v.bn_stats(bnbuf[:, 0:6], src[:, 0:512])
                v.bn_stats(bnbuf[:, 6:12], src[:, 512:1024])
                v.drain()
                return v.bn_aggr(bnbuf[:, 12:14], bnbuf[:, 0:12])
            t_bn = stage("vector", f_bn, deps=deps)
            t_sq = stage("scalar", lambda s: s.activation(bnbuf[:, 14:15], bnbuf[:, 13:14], AF.Sqrt,
                                                          bias=epsc[:], scale=1.0), deps=[t_bn])

            def f_rs(v):
                v.reciprocal(tcol[:, 2:3], bnbuf[:, 14:15])
                v.tensor_copy(tcol[:, 1:2], bnbuf[:, 12:13])
                v.drain()
                return v.tensor_scalar(tcol[:, 0:1], tcol[:, 1:2], tcol[:, 2:3], -1.0,
                                       op0=OP.mult, op1=OP.mult)
            t_rs = stage("vector", f_rs, deps=[t_sq])
            if save_i is not None:
                stage("vector", lambda v: v.tensor_copy(MRS[:, 4 * save_i:4 * save_i + 2], tcol[:, 1:3]),
                      deps=[t_rs])
            return stage("scalar", lambda s: s.activation(hdst, src, AF.Identity,
                                                          scale=tcol[:, 2:3], bias=tcol[:, 0:1]),
                         deps=[t_rs])

        last_tsp = [None, None]

        def emit_transpose_split(srcf, dst_hi, dst_lo, col_of, deps=None):
            def f_tr(t):
                last = None
                for c in range(CC):
                    bank = PS[6 + c // 4]
                    last = t.transpose(bank[:, 128 * (c % 4):128 * (c % 4) + 128],
                                       srcf[:, 128 * c:128 * (c + 1)], iden[:])
                return last
            dtr = None if deps is None else (list(deps) + [t for t in last_tsp if t])
            t_tr = stage("tensor", f_tr, deps=dtr)

            def f_hi(s):
                last = None
                for c in range(CC):
                    bank = PS[6 + c // 4]
                    last = s.copy(dst_hi[:, col_of(c):col_of(c) + 128],
                                  bank[:, 128 * (c % 4):128 * (c % 4) + 128])
                return last
            t_hi = stage("scalar", f_hi, deps=[t_tr])

            def f_lo(v):
                last = None
                for c in range(CC):
                    bank = PS[6 + c // 4]
                    last = v.tensor_tensor(dst_lo[:, col_of(c):col_of(c) + 128],
                                           bank[:, 128 * (c % 4):128 * (c % 4) + 128],
                                           dst_hi[:, col_of(c):col_of(c) + 128], op=OP.subtract)
                return last
            t_lo = stage("vector", f_lo, deps=[t_hi])
            last_tsp[0], last_tsp[1] = t_hi, t_lo
            return (t_tr, t_hi, t_lo)

        def emit_mm3(t, pslice, lhsT_hi, lhsT_lo, rhs_hi, rhs_lo, nk=CC):
            last = None
            for c in range(nk):
                last = t.matmul(pslice, lhsT_hi(c), rhs_hi(c), start=(c == 0), stop=False)
                last = t.matmul(pslice, lhsT_hi(c), rhs_lo(c), start=False, stop=False)
                last = t.matmul(pslice, lhsT_lo(c), rhs_hi(c), start=False, stop=(c == nk - 1))
            return last

        def emit_bcast_row(src_row, dst, rowbuf):
            dmas([(rowbuf[:], src_row)])

            def f_mm(t):
                t.matmul(PS[0][:, 0:512], ones1[:], rowbuf[:, 0:512], start=True, stop=True)
                return t.matmul(PS[1][:, 0:512], ones1[:], rowbuf[:, 512:1024], start=True, stop=True)
            stage("tensor", f_mm)

            def f_ev(s):
                s.copy(dst[:, 0:512], PS[0][:, 0:512])
                return s.copy(dst[:, 512:1024], PS[1][:, 0:512])
            stage("scalar", f_ev)

        def emit_load_split_w(src, whi, wlo, wldf):
            for c in range(CC):
                dmas([(wldf[:], src[128 * c:128 * (c + 1), :])])

                def f(v, c=c):
                    v.tensor_copy(whi[:, DM * c:DM * (c + 1)], wldf[:])
                    v.drain()
                    return v.tensor_tensor(wlo[:, DM * c:DM * (c + 1)], wldf[:],
                                           whi[:, DM * c:DM * (c + 1)], op=OP.subtract)
                stage("vector", f)

        with contextlib.ExitStack() as p0:
            ibuf = p0.enter_context(nc.sbuf_tensor("p0_ibuf", [128, HID], I32))
            bbuf = p0.enter_context(nc.sbuf_tensor("p0_bbuf", [128, HID], BF16))
            wf = p0.enter_context(nc.sbuf_tensor("p0_wf", [128, DI2], F32))
            sbh = p0.enter_context(nc.sbuf_tensor("p0_sbh", [128, DI2], BF16))
            sbl = p0.enter_context(nc.sbuf_tensor("p0_sbl", [128, DI2], BF16))

            dmas([(Xt(i), A["x"][128 * i:128 * (i + 1), :]) for i in range(TT)])
            dmas([(iden[:], A["iden"])])
            stage("vector", lambda v: v.memset(ones1[:], 1.0))
            stage("vector", lambda v: v.memset(epsc[:], 1e-5))

            for src, dst, rows, width in ((A["wgT"], st_wg, CC, HID), (A["wuT"], st_wu, CC, HID),
                                          (A["wdT"], st_wd, HC, DM)):
                for j in range(rows):
                    dmas([(ibuf[:, 0:width], src[128 * j:128 * (j + 1), :])])
                    stage("vector", lambda v, width=width: v.tensor_copy(bbuf[:, 0:width], ibuf[:, 0:width]))
                    dmas([(dst[128 * j:128 * (j + 1), :], bbuf[:, 0:width])])

            for j in range(CC):
                dmas([(wf[:], A["winT"][128 * j:128 * (j + 1), :])])

                def f_sp(v):
                    v.tensor_copy(sbh[:], wf[:])
                    v.drain()
                    return v.tensor_tensor(sbl[:], wf[:], sbh[:], op=OP.subtract)
                stage("vector", f_sp)
                dmas([(st_wih[o, :, j, :], sbh[:, 128 * o:128 * (o + 1)]) for o in range(OC)]
                     + [(st_wil[o, :, j, :], sbl[:, 128 * o:128 * (o + 1)]) for o in range(OC)])

            for j in range(DC):
                dmas([(wf[:, 0:DM], A["woutT"][128 * j:128 * (j + 1), :])])

                def f_sp2(v):
                    v.tensor_copy(sbh[:, 0:DM], wf[:, 0:DM])
                    v.drain()
                    return v.tensor_tensor(sbl[:, 0:DM], wf[:, 0:DM], sbh[:, 0:DM], op=OP.subtract)
                stage("vector", f_sp2)
                dmas([(st_woh[128 * j:128 * (j + 1), :], sbh[:, 0:DM]),
                      (st_wol[128 * j:128 * (j + 1), :], sbl[:, 0:DM])])

        with contextlib.ExitStack() as pat:
            hbuf2 = pat.enter_context(nc.sbuf_tensor("hbuf2", [128, DM], F32))
            tsp_hi = pat.enter_context(nc.sbuf_tensor("tsp_hi", [128, DM], BF16))
            tsp_lo = pat.enter_context(nc.sbuf_tensor("tsp_lo", [128, DM], BF16))
            rowbuf = pat.enter_context(nc.sbuf_tensor("rowbuf", [1, DM], F32))
            normw_b = pat.enter_context(nc.sbuf_tensor("normw_b", [128, DM], F32))
            normb_b = pat.enter_context(nc.sbuf_tensor("normb_b", [128, DM], F32))

            emit_bcast_row(A["normw"], normw_b, rowbuf)
            emit_bcast_row(A["normb"], normb_b, rowbuf)

            def emit_h1(i, save):
                if save:
                    emit_ln(Xt(i), hbuf1, save_i=i)
                    emit_ln(hbuf1, hbuf2)
                    stage("vector", lambda v: v.tensor_copy(MRS[:, 4 * i + 2:4 * i + 4], tcol[:, 1:3]))
                else:
                    def f_b0(v):
                        return v.tensor_scalar(tcol[:, 0:1], MRS[:, 4 * i:4 * i + 1],
                                               MRS[:, 4 * i + 1:4 * i + 2], -1.0, op0=OP.mult, op1=OP.mult)
                    stage("vector", f_b0)
                    stage("scalar", lambda s: s.activation(hbuf1[:], Xt(i), AF.Identity,
                                                           scale=MRS[:, 4 * i + 1:4 * i + 2], bias=tcol[:, 0:1]))

                    def f_b2(v):
                        return v.tensor_scalar(tcol[:, 3:4], MRS[:, 4 * i + 2:4 * i + 3],
                                               MRS[:, 4 * i + 3:4 * i + 4], -1.0, op0=OP.mult, op1=OP.mult)
                    stage("vector", f_b2)
                    stage("scalar", lambda s: s.activation(hbuf2[:], hbuf1[:], AF.Identity,
                                                           scale=MRS[:, 4 * i + 3:4 * i + 4], bias=tcol[:, 3:4]))

                def f_aff(v):
                    v.tensor_tensor(hbuf1[:], hbuf2[:], normw_b[:], op=OP.mult)
                    v.drain()
                    return v.tensor_tensor(hbuf1[:], hbuf1[:], normb_b[:], op=OP.add)
                stage("vector", f_aff)

            with contextlib.ExitStack() as pa:
                bk_b = pa.enter_context(nc.sbuf_tensor("bk_b", [128, DM], F32))
                bv_b = pa.enter_context(nc.sbuf_tensor("bv_b", [128, DM], F32))
                wk_hi = pa.enter_context(nc.sbuf_tensor("wk_hi", [128, CC * DM], BF16))
                wk_lo = pa.enter_context(nc.sbuf_tensor("wk_lo", [128, CC * DM], BF16))
                wv_hi = pa.enter_context(nc.sbuf_tensor("wv_hi", [128, CC * DM], BF16))
                wv_lo = pa.enter_context(nc.sbuf_tensor("wv_lo", [128, CC * DM], BF16))
                wldf = pa.enter_context(nc.sbuf_tensor("wldf_a", [128, DM], F32))
                kv_hi = pa.enter_context(nc.sbuf_tensor("kv_hi", [128, DM], BF16))
                kv_lo = pa.enter_context(nc.sbuf_tensor("kv_lo", [128, DM], BF16))

                emit_bcast_row(A["bk"], bk_b, rowbuf)
                emit_bcast_row(A["bv"], bv_b, rowbuf)
                emit_load_split_w(A["wkT"], wk_hi, wk_lo, wldf)
                emit_load_split_w(A["wvT"], wv_hi, wv_lo, wldf)

                for i in range(TT):
                    emit_h1(i, save=True)
                    emit_transpose_split(hbuf1, tsp_hi, tsp_lo, lambda c: 128 * c)
                    for whi, wlo, bias_b, sth, stl in (
                            (wk_hi, wk_lo, bk_b, st_kh, st_kl),
                            (wv_hi, wv_lo, bv_b, st_vh, st_vl)):
                        for half in range(2):
                            def f_mm(t, whi=whi, wlo=wlo, half=half):
                                return emit_mm3(
                                    t, PS[half][:, 0:512],
                                    lambda c: tsp_hi[:, 128 * c:128 * (c + 1)],
                                    lambda c: tsp_lo[:, 128 * c:128 * (c + 1)],
                                    lambda c: whi[:, DM * c + 512 * half:DM * c + 512 * (half + 1)],
                                    lambda c: wlo[:, DM * c + 512 * half:DM * c + 512 * (half + 1)])
                            stage("tensor", f_mm)

                            def f_ev(v, half=half, bias_b=bias_b):
                                return v.tensor_tensor(hbuf2[:, 512 * half:512 * (half + 1)],
                                                       PS[half][:, 0:512],
                                                       bias_b[:, 512 * half:512 * (half + 1)], op=OP.add)
                            stage("vector", f_ev)

                        def f_sp(v):
                            v.tensor_copy(kv_hi[:], hbuf2[:])
                            v.drain()
                            return v.tensor_tensor(kv_lo[:], hbuf2[:], kv_hi[:], op=OP.subtract)
                        stage("vector", f_sp)
                        dmas([(sth[128 * i:128 * (i + 1), :], kv_hi[:]),
                              (stl[128 * i:128 * (i + 1), :], kv_lo[:])])

            with contextlib.ExitStack() as pa:
                S_hi = pa.enter_context(nc.sbuf_tensor("S_hi", [128, CC * DM], BF16))
                S_lo = pa.enter_context(nc.sbuf_tensor("S_lo", [128, CC * DM], BF16))
                with contextlib.ExitStack() as ps_:
                    kt_hi = ps_.enter_context(nc.sbuf_tensor("kt_hi", [128, DM], BF16))
                    kt_lo = ps_.enter_context(nc.sbuf_tensor("kt_lo", [128, DM], BF16))
                    vt_hi = ps_.enter_context(nc.sbuf_tensor("vt_hi", [128, DM], BF16))
                    vt_lo = ps_.enter_context(nc.sbuf_tensor("vt_lo", [128, DM], BF16))
                    for r in range(2):
                        for i in range(TT):
                            dmas([(kt_hi[:], st_kh[128 * i:128 * (i + 1), :]),
                                  (kt_lo[:], st_kl[128 * i:128 * (i + 1), :]),
                                  (vt_hi[:], st_vh[128 * i:128 * (i + 1), :]),
                                  (vt_lo[:], st_vl[128 * i:128 * (i + 1), :])])

                            def f_mm(t, i=i, r=r):
                                last = None
                                for c1 in range(CC):
                                    p = PS[c1][:, 0:512]
                                    last = t.matmul(p, kt_hi[:, 128 * c1:128 * (c1 + 1)],
                                                    vt_hi[:, 512 * r:512 * (r + 1)],
                                                    start=(i == 0), stop=False)
                                    last = t.matmul(p, kt_hi[:, 128 * c1:128 * (c1 + 1)],
                                                    vt_lo[:, 512 * r:512 * (r + 1)], start=False, stop=False)
                                    last = t.matmul(p, kt_lo[:, 128 * c1:128 * (c1 + 1)],
                                                    vt_hi[:, 512 * r:512 * (r + 1)],
                                                    start=False, stop=(i == TT - 1))
                                return last
                            stage("tensor", f_mm)

                        def f_evh(s, r=r):
                            last = None
                            for c1 in range(CC):
                                last = s.copy(S_hi[:, DM * c1 + 512 * r:DM * c1 + 512 * (r + 1)],
                                              PS[c1][:, 0:512])
                            return last
                        stage("scalar", f_evh)

                        def f_evl(v, r=r):
                            last = None
                            for c1 in range(CC):
                                last = v.tensor_tensor(S_lo[:, DM * c1 + 512 * r:DM * c1 + 512 * (r + 1)],
                                                       PS[c1][:, 0:512],
                                                       S_hi[:, DM * c1 + 512 * r:DM * c1 + 512 * (r + 1)],
                                                       op=OP.subtract)
                            return last
                        stage("vector", f_evl)

                with contextlib.ExitStack() as pb:
                    bq_b = pb.enter_context(nc.sbuf_tensor("bq_b", [128, DM], F32))
                    bp_b = pb.enter_context(nc.sbuf_tensor("bp_b", [128, DM], F32))
                    wq_hi = pb.enter_context(nc.sbuf_tensor("wq_hi", [128, CC * DM], BF16))
                    wq_lo = pb.enter_context(nc.sbuf_tensor("wq_lo", [128, CC * DM], BF16))
                    wp_hi = pb.enter_context(nc.sbuf_tensor("wp_hi", [128, CC * DM], BF16))
                    wp_lo = pb.enter_context(nc.sbuf_tensor("wp_lo", [128, CC * DM], BF16))
                    wldf = pb.enter_context(nc.sbuf_tensor("wldf_b", [128, DM], F32))

                    emit_bcast_row(A["bq"], bq_b, rowbuf)
                    emit_bcast_row(A["bp"], bp_b, rowbuf)
                    emit_load_split_w(A["wqT"], wq_hi, wq_lo, wldf)
                    emit_load_split_w(A["wpT"], wp_hi, wp_lo, wldf)

                    def mm_pair(bank_base, whi, wlo, evac):
                        for half in range(2):
                            def f_mm(t, half=half):
                                return emit_mm3(
                                    t, PS[bank_base + half][:, 0:512],
                                    lambda c: tsp_hi[:, 128 * c:128 * (c + 1)],
                                    lambda c: tsp_lo[:, 128 * c:128 * (c + 1)],
                                    lambda c: whi[:, DM * c + 512 * half:DM * c + 512 * (half + 1)],
                                    lambda c: wlo[:, DM * c + 512 * half:DM * c + 512 * (half + 1)])
                            stage("tensor", f_mm)
                            evac(half)

                    for i in range(TT):
                        emit_h1(i, save=False)
                        emit_transpose_split(hbuf1, tsp_hi, tsp_lo, lambda c: 128 * c)

                        def ev_q(half):
                            def f(v, half=half):
                                return v.tensor_tensor(hbuf2[:, 512 * half:512 * (half + 1)],
                                                       PS[half][:, 0:512],
                                                       bq_b[:, 512 * half:512 * (half + 1)], op=OP.add)
                            stage("vector", f)
                        mm_pair(0, wq_hi, wq_lo, ev_q)
                        emit_transpose_split(hbuf2, tsp_hi, tsp_lo, lambda c: 128 * c)

                        def ev_o1(half):
                            stage("scalar", lambda s, half=half: s.activation(
                                hbuf1[:, 512 * half:512 * (half + 1)], PS[2 + half][:, 0:512],
                                AF.Identity, scale=float(1.0 / 32.0)))
                        for half in range(2):
                            def f_mm(t, half=half):
                                return emit_mm3(
                                    t, PS[2 + half][:, 0:512],
                                    lambda c: tsp_hi[:, 128 * c:128 * (c + 1)],
                                    lambda c: tsp_lo[:, 128 * c:128 * (c + 1)],
                                    lambda c: S_hi[:, DM * c + 512 * half:DM * c + 512 * (half + 1)],
                                    lambda c: S_lo[:, DM * c + 512 * half:DM * c + 512 * (half + 1)])
                            stage("tensor", f_mm)
                            ev_o1(half)
                        emit_transpose_split(hbuf1, tsp_hi, tsp_lo, lambda c: 128 * c)

                        def ev_p(half):
                            def f(v, half=half):
                                return v.tensor_tensor(hbuf2[:, 512 * half:512 * (half + 1)],
                                                       PS[4 + half][:, 0:512],
                                                       bp_b[:, 512 * half:512 * (half + 1)], op=OP.add)
                            stage("vector", f)
                        for half in range(2):
                            def f_mm(t, half=half):
                                return emit_mm3(
                                    t, PS[4 + half][:, 0:512],
                                    lambda c: tsp_hi[:, 128 * c:128 * (c + 1)],
                                    lambda c: tsp_lo[:, 128 * c:128 * (c + 1)],
                                    lambda c: wp_hi[:, DM * c + 512 * half:DM * c + 512 * (half + 1)],
                                    lambda c: wp_lo[:, DM * c + 512 * half:DM * c + 512 * (half + 1)])
                            stage("tensor", f_mm)
                            ev_p(half)

                        def f_b0(v, i=i):
                            return v.tensor_scalar(tcol[:, 0:1], MRS[:, 4 * i:4 * i + 1],
                                                   MRS[:, 4 * i + 1:4 * i + 2], -1.0,
                                                   op0=OP.mult, op1=OP.mult)
                        stage("vector", f_b0)
                        stage("scalar", lambda s, i=i: s.activation(hbuf1[:], Xt(i), AF.Identity,
                                                                    scale=MRS[:, 4 * i + 1:4 * i + 2],
                                                                    bias=tcol[:, 0:1]))

                        def f_res(v, i=i):
                            v.tensor_tensor(hbuf2[:], hbuf2[:], hbuf1[:], op=OP.add)
                            v.drain()
                            return v.tensor_tensor(Xt(i), Xt(i), hbuf2[:], op=OP.add)
                        stage("vector", f_res)

            if debug:
                dmas([(dbg["dbg_attn"][128 * i:128 * (i + 1), :], Xt(i)) for i in range(TT)])

        def emit_ffn(tag):
            with contextlib.ExitStack() as pf:
                xlnT_hi = pf.enter_context(nc.sbuf_tensor(f"xlnT_hi_{tag}", [128, CC * T], BF16))
                xlnT_lo = pf.enter_context(nc.sbuf_tensor(f"xlnT_lo_{tag}", [128, CC * T], BF16))
                wg_q = pf.enter_context(nc.sbuf_tensor(f"wg_q_{tag}", [128, CC * 768], BF16))
                wu_q = pf.enter_context(nc.sbuf_tensor(f"wu_q_{tag}", [128, CC * 768], BF16))
                wd_q = pf.enter_context(nc.sbuf_tensor(f"wd_q_{tag}", [128, 6 * DM], BF16))
                sg = pf.enter_context(nc.sbuf_tensor(f"sg_{tag}", [128, 256], F32))
                hhf = pf.enter_context(nc.sbuf_tensor(f"hhf_{tag}", [128, 256], F32))
                hh_hi = pf.enter_context(nc.sbuf_tensor(f"hh_hi_{tag}", [128, 6 * 256], BF16))
                hh_lo = pf.enter_context(nc.sbuf_tensor(f"hh_lo_{tag}", [128, 6 * 256], BF16))

                for i in range(TT):
                    emit_ln(Xt(i), hbuf1)
                    emit_transpose_split(hbuf1, xlnT_hi, xlnT_lo, lambda c, i=i: T * c + 128 * i)

                for q in range(4):
                    dmas([(wg_q[:].rearrange("p (c h) -> p c h", h=768),
                           st_wg.rearrange("(c p) h -> p c h", p=128)[:, :, 768 * q:768 * (q + 1)]),
                          (wu_q[:].rearrange("p (c h) -> p c h", h=768),
                           st_wu.rearrange("(c p) h -> p c h", p=128)[:, :, 768 * q:768 * (q + 1)])])
                    dmas([(wd_q[:, DM * h:DM * (h + 1)],
                           st_wd[128 * (6 * q + h):128 * (6 * q + h + 1), :]) for h in range(6)])

                    for sl in range(NSC):
                        t0 = 256 * sl
                        for h in range(6):
                            def f_g(t, h=h, t0=t0):
                                last = None
                                for wq_, bank in ((wg_q, PS[0]), (wu_q, PS[1])):
                                    for c in range(CC):
                                        last = t.matmul(bank[:, 0:256],
                                                        wq_[:, 768 * c + 128 * h:768 * c + 128 * (h + 1)],
                                                        xlnT_hi[:, T * c + t0:T * c + t0 + 256],
                                                        start=(c == 0), stop=False)
                                        last = t.matmul(bank[:, 0:256],
                                                        wq_[:, 768 * c + 128 * h:768 * c + 128 * (h + 1)],
                                                        xlnT_lo[:, T * c + t0:T * c + t0 + 256],
                                                        start=False, stop=(c == CC - 1))
                                return last
                            stage("tensor", f_g)
                            stage("scalar", lambda s: s.activation(sg[:], PS[0][:, 0:256],
                                                                   AF.Silu, scale=S127))

                            def f_hh(v, h=h):
                                v.tensor_tensor(hhf[:], sg[:], PS[1][:, 0:256], op=OP.mult)
                                v.drain()
                                v.tensor_copy(hh_hi[:, 256 * h:256 * (h + 1)], hhf[:])
                                v.drain()
                                return v.tensor_tensor(hh_lo[:, 256 * h:256 * (h + 1)], hhf[:],
                                                       hh_hi[:, 256 * h:256 * (h + 1)], op=OP.subtract)
                            stage("vector", f_hh)

                            def f_d(t, h=h):
                                last = None
                                for tsub in range(2):
                                    for ch in range(2):
                                        bank = PS[2 + 2 * tsub + ch]
                                        last = t.matmul(
                                            bank[:, 0:512],
                                            hh_hi[:, 256 * h + 128 * tsub:256 * h + 128 * (tsub + 1)],
                                            wd_q[:, DM * h + 512 * ch:DM * h + 512 * (ch + 1)],
                                            start=(h == 0), stop=False)
                                        last = t.matmul(
                                            bank[:, 0:512],
                                            hh_lo[:, 256 * h + 128 * tsub:256 * h + 128 * (tsub + 1)],
                                            wd_q[:, DM * h + 512 * ch:DM * h + 512 * (ch + 1)],
                                            start=False, stop=(h == 5))
                                return last
                            stage("tensor", f_d)

                        def f_acc(v, sl=sl):
                            last = None
                            for tsub in range(2):
                                i = 2 * sl + tsub
                                for ch in range(2):
                                    bank = PS[2 + 2 * tsub + ch]
                                    xs = X[:, DM * i + 512 * ch:DM * i + 512 * (ch + 1)]
                                    last = v.scalar_tensor_tensor(xs, bank[:, 0:512], S127SQ, xs,
                                                                  op0=OP.mult, op1=OP.add)
                            return last
                        stage("vector", f_acc)

                if debug and tag == "f1":
                    dmas([(dbg["dbg_ffn1"][128 * i:128 * (i + 1), :], Xt(i)) for i in range(TT)])

        emit_ffn("f1")

        with contextlib.ExitStack() as pm:
            A_r = pm.enter_context(nc.sbuf_tensor("A_r", [128, DC * NST], F32))
            D_r = pm.enter_context(nc.sbuf_tensor("D_r", [128, DC], F32))
            bconv_r = pm.enter_context(nc.sbuf_tensor("bconv_r", [128, DC], F32))
            bdt_r = pm.enter_context(nc.sbuf_tensor("bdt_r", [128, DC], F32))
            wconv_r = pm.enter_context(nc.sbuf_tensor("wconv_r", [128, DC * 4], F32))
            wxr_s = pm.enter_context(nc.sbuf_tensor("wxr_s", [128, DC * 96], F32))
            sel_s = pm.enter_context(nc.sbuf_tensor("sel_s", [NST, NST * 128], F32))
            xcb = pm.enter_context(nc.sbuf_tensor("xcb", [128, DC * 259], F32))
            ub = pm.enter_context(nc.sbuf_tensor("ub", [128, DC * 256], F32))
            deltab = pm.enter_context(nc.sbuf_tensor("deltab", [128, DC * 256], F32))
            gb = pm.enter_context(nc.sbuf_tensor("gb", [128, DC * 256], F32))
            sres = pm.enter_context(nc.sbuf_tensor("sres", [128, DC * 256], F32))
            halob = pm.enter_context(nc.sbuf_tensor("halob", [128, DC * 3], F32))
            scan_init = pm.enter_context(nc.sbuf_tensor("scan_init", [128, DC * NST], F32))
            Bb = pm.enter_context(nc.sbuf_tensor("Bb", [128, 2 * 256], F32))
            Cb = pm.enter_context(nc.sbuf_tensor("Cb", [128, 2 * 256], F32))
            dt_sb = pm.enter_context(nc.sbuf_tensor("dt_sb", [DTR, 256], F32))
            B_sb = pm.enter_context(nc.sbuf_tensor("B_sb", [NST, 256], F32))
            C_sb = pm.enter_context(nc.sbuf_tensor("C_sb", [NST, 256], F32))
            wob_hi = pm.enter_context(nc.sbuf_tensor("wob_hi", [128, 2 * DM], BF16))
            wob_lo = pm.enter_context(nc.sbuf_tensor("wob_lo", [128, 2 * DM], BF16))

            dmas([(A_r[:], A["alog"]), (D_r[:], A["dvec"]), (bconv_r[:], A["bconv"]),
                  (bdt_r[:], A["bdt"]), (wconv_r[:], A["wconv"]), (wxr_s[:], A["wxr"]),
                  (sel_s[:], A["sel"])])
            stage("scalar", lambda s: s.activation(A_r[:], A_r[:], AF.Exp, scale=1.0))
            stage("vector", lambda v: v.tensor_scalar(A_r[:], A_r[:], -1.0, None, op0=OP.mult))
            stage("vector", lambda v: v.memset(halob[:], 0.0))
            stage("vector", lambda v: v.memset(scan_init[:], 0.0))

            xcb3 = xcb[:].rearrange("p (c k) -> p c k", k=259)
            dA = xcb[:, 0:DC * 129]
            dBu = xcb[:, DC * 129:DC * 258]
            dA3 = dA.rearrange("p (c k) -> p c k", k=129)
            dBu3 = dBu.rearrange("p (c k) -> p c k", k=129)
            si3 = scan_init[:].rearrange("p (c n) -> p c n", n=NST)
            gb3 = gb[:].rearrange("p (c k) -> p c k", k=256)
            ub3 = ub[:].rearrange("p (c k) -> p c k", k=256)

            fresh = [None]

            def mstage(eng, fn, deps, **kw):
                if eng not in fresh[0]:
                    fresh[0].add(eng)
                    return stage(eng, fn, deps=None, **kw)
                return stage(eng, fn, deps=[d for d in deps if d], **kw)

            def mdmas(pairs, sem, deps):
                if "sync" not in fresh[0]:
                    fresh[0].add("sync")
                    return dmas(pairs, sem=sem, deps=None)
                return dmas(pairs, sem=sem, deps=[d for d in deps if d])

            for sc in range(NSC):
                i0 = 2 * sc
                fresh[0] = set()
                bfree = {b: [] for b in range(8)}

                pfr = contextlib.ExitStack()
                xlnT_hi = pfr.enter_context(nc.sbuf_tensor(f"xlnT_hi_m{sc}", [128, CC * 256], BF16))
                xlnT_lo = pfr.enter_context(nc.sbuf_tensor(f"xlnT_lo_m{sc}", [128, CC * 256], BF16))
                wib_hi = pfr.enter_context(nc.sbuf_tensor(f"wib_hi{sc}", [128, 2 * CC * 128], BF16))
                wib_lo = pfr.enter_context(nc.sbuf_tensor(f"wib_lo{sc}", [128, 2 * CC * 128], BF16))
                wdtT_s = pfr.enter_context(nc.sbuf_tensor(f"wdtT_s{sc}", [DTR, DI], F32))
                z1 = pfr.enter_context(nc.sbuf_tensor(f"z1_{sc}", [128, 2 * 256], F32))
                t_wdt = mdmas([(wdtT_s[:], A["wdtT"])], sem="lmisc", deps=[None])

                t_lo_all = []
                for tl in range(2):
                    dep0 = None if tl == 0 else [t_lo_all[-1]]
                    t_ap = emit_ln(Xt(i0 + tl), hbuf1, deps=dep0)
                    fresh[0].update(("vector", "scalar"))
                    ttr, thi, tlo = emit_transpose_split(hbuf1, xlnT_hi, xlnT_lo,
                                                         lambda c, tl=tl: 256 * c + 128 * tl,
                                                         deps=[t_ap])
                    fresh[0].add("tensor")
                    t_lo_all.append(tlo)
                bfree[6] = [thi, tlo]
                bfree[7] = [thi, tlo]
                t_xln = [thi, tlo]

                t_wimm = [None] * OC
                t_wevac = [None] * OC
                t_wld = [None] * OC
                for o in range(OC):
                    sl = o % 2
                    t_wld[o] = mdmas(
                        [(wib_hi[:, sl * DM:sl * DM + DM].rearrange("p (c k) -> p c k", k=128), st_wih[o]),
                         (wib_lo[:, sl * DM:sl * DM + DM].rearrange("p (c k) -> p c k", k=128), st_wil[o])],
                        sem=f"lwi{sl}", deps=[t_wimm[o - 2] if o >= 2 else None])
                    bank = o % 4

                    def f_mm(t, o=o, sl=sl, bank=bank):
                        return emit_mm3(
                            t, PS[bank][:, 0:256],
                            lambda c: wib_hi[:, sl * DM + 128 * c:sl * DM + 128 * (c + 1)],
                            lambda c: wib_lo[:, sl * DM + 128 * c:sl * DM + 128 * (c + 1)],
                            lambda c: xlnT_hi[:, 256 * c:256 * (c + 1)],
                            lambda c: xlnT_lo[:, 256 * c:256 * (c + 1)])
                    t_wimm[o] = mstage("tensor", f_mm,
                                       deps=[t_wld[o]] + t_xln + bfree[bank])
                    if o < DC:
                        t_wevac[o] = mstage("scalar", lambda s, o=o, bank=bank: s.copy(
                            xcb[:, 259 * o + 3:259 * (o + 1)], PS[bank][:, 0:256]),
                            deps=[t_wimm[o]])
                    else:
                        t_wevac[o] = mstage("scalar", lambda s, o=o, bank=bank: s.activation(
                            sres[:, 256 * (o - DC):256 * (o - DC + 1)], PS[bank][:, 0:256], AF.Silu),
                            deps=[t_wimm[o]])
                    bfree[bank] = [t_wevac[o]]

                t_hin = mstage("vector", lambda v: v.tensor_copy(
                    xcb3[:, :, 0:3], halob[:].rearrange("p (c k) -> p c k", k=3)),
                    deps=[t_wevac[DC - 1]])
                t_cv = None
                for j in range(4):
                    def f_cv(v, j=j):
                        last = None
                        for c in range(DC):
                            acc = ub[:, 256 * c:256 * (c + 1)]
                            xs = xcb[:, 259 * c + j:259 * c + j + 256]
                            wj = wconv_r[:, 4 * c + j:4 * c + j + 1]
                            if j == 0:
                                last = v.tensor_scalar(acc, xs, wj, None, op0=OP.mult)
                            else:
                                last = v.scalar_tensor_tensor(acc, xs, wj, acc, op0=OP.mult, op1=OP.add)
                        return last
                    t_cv = mstage("vector", f_cv,
                                  deps=[t_hin, t_wevac[DC - 1], ("ve", sems["ve"][1])])
                t_hsv = mstage("vector", lambda v: v.tensor_copy(
                    halob[:].rearrange("p (c k) -> p c k", k=3), xcb3[:, :, 256:259]),
                    deps=[t_cv])

                def f_u(s):
                    last = None
                    for c in range(DC):
                        last = s.activation(ub[:, 256 * c:256 * (c + 1)], ub[:, 256 * c:256 * (c + 1)],
                                            AF.Silu, bias=bconv_r[:, c:c + 1], scale=1.0)
                    return last
                t_u = mstage("scalar", f_u, deps=[t_cv])

                def f_xp(t):
                    last = None
                    for lo, hi, bank, rows in ((0, 64, PS[4], DTR), (64, 80, PS[5], NST), (80, 96, PS[6], NST)):
                        for c in range(DC):
                            last = t.matmul(bank[0:rows, 0:256], wxr_s[:, 96 * c + lo:96 * c + hi],
                                            ub[:, 256 * c:256 * (c + 1)],
                                            start=(c == 0), stop=(c == DC - 1))
                    return last
                t_xp = mstage("tensor", f_xp, deps=[t_u] + bfree[4] + bfree[5] + bfree[6])

                def f_xpe(s):
                    s.copy(dt_sb[:], PS[4][0:DTR, 0:256])
                    s.copy(B_sb[:], PS[5][0:NST, 0:256])
                    return s.copy(C_sb[:], PS[6][0:NST, 0:256])
                t_xpe = mstage("scalar", f_xpe, deps=[t_xp])
                bfree[4] = bfree[5] = bfree[6] = [t_xpe]

                t_dmm = [None] * DC
                t_raw = [None] * DC
                for c in range(DC):
                    bank = c % 4
                    t_dmm[c] = mstage("tensor", lambda t, c=c, bank=bank: t.matmul(
                        PS[bank][:, 0:256], wdtT_s[:, 128 * c:128 * (c + 1)], dt_sb[:],
                        start=True, stop=True),
                        deps=[t_xpe, t_wdt] + bfree[bank])
                    t_raw[c] = mstage("vector", lambda v, c=c, bank=bank: v.tensor_scalar(
                        deltab[:, 256 * c:256 * (c + 1)], PS[bank][:, 0:256],
                        bdt_r[:, c:c + 1], None, op0=OP.add),
                        deps=[t_dmm[c]])
                    bfree[bank] = [t_raw[c]]
                t_ab = mstage("scalar", lambda s: s.activation(gb[:], deltab[:], AF.Abs, scale=1.0),
                              deps=[t_raw[DC - 1]])
                t_ex = mstage("scalar", lambda s: s.activation(gb[:], gb[:], AF.Exp, scale=-1.0),
                              deps=[t_ab])
                t_ln = mstage("scalar", lambda s: s.activation(gb[:], gb[:], AF.Ln, bias=1.0, scale=1.0),
                              deps=[t_ex])
                t_m = mstage("vector", lambda v: v.tensor_scalar(
                    deltab[:], deltab[:], 0.0, None, op0=OP.max), deps=[t_ab])
                t_fadd = mstage("vector", lambda v: v.tensor_tensor(deltab[:], deltab[:], gb[:], op=OP.add),
                                deps=[t_ln, t_m])

                t_g = mstage("vector", lambda v: v.tensor_tensor(gb[:], deltab[:], ub[:], op=OP.mult),
                             deps=[t_u, t_fadd])

                def f_ud(v):
                    last = None
                    for c in range(DC):
                        last = v.tensor_scalar(ub[:, 256 * c:256 * (c + 1)], ub[:, 256 * c:256 * (c + 1)],
                                               D_r[:, c:c + 1], None, op0=OP.mult)
                    return last
                t_ud = mstage("vector", f_ud, deps=[t_xp, t_g])
                pfr.close()

                pn = contextlib.ExitStack()
                tmpb = pn.enter_context(nc.sbuf_tensor(f"tmpb_{sc}", [128, DC * 128], F32))
                sl1 = pn.enter_context(nc.sbuf_tensor(f"sl1_{sc}", [128, 2 * DC * 129], F32))
                tmpb3 = tmpb[:].rearrange("p (c k) -> p c k", k=128)
                A3 = A_r[:].rearrange("p (c n) -> p c n", n=NST)
                dA_s = [dA, sl1[:, 0:DC * 129]]
                dBu_s = [dBu, sl1[:, DC * 129:2 * DC * 129]]
                dA3_s = [d.rearrange("p (c k) -> p c k", k=129) for d in dA_s]
                dBu3_s = [d.rearrange("p (c k) -> p c k", k=129) for d in dBu_s]
                t_ms = mstage("vector", lambda v: v.memset(dA3_s[0][:, :, 0:1], 0.0), deps=[t_hsv])
                t_ms = mstage("vector", lambda v: v.memset(dA3_s[1][:, :, 0:1], 0.0), deps=[t_ud])

                U = 2 * NST
                t_dam = [None] * U
                t_dae = [None] * U
                t_scn = [None] * U
                t_ym = [None] * U
                t_bcastk = [None, None]
                for u in range(U + 1):
                    n, sub = u // 2, u % 2
                    slp = u % 2
                    if u < U:
                        if sub == 0:
                            pslot = n % 2
                            pbank = 6 + pslot

                            def f_bc(t, n=n, pbank=pbank):
                                t.matmul(PS[pbank][:, 0:256], sel_s[:, 128 * n:128 * (n + 1)],
                                         B_sb[:], start=True, stop=True)
                                return t.matmul(PS[pbank][:, 256:512], sel_s[:, 128 * n:128 * (n + 1)],
                                                C_sb[:], start=True, stop=True)
                            t_bcastk[pslot] = mstage("tensor", f_bc,
                                                     deps=[t_xpe,
                                                           t_ym[2 * (n - 2) + 1] if n >= 2 else None]
                                                          + bfree[pbank])
                            bfree[pbank] = []

                        def f_dam(v, n=n, sub=sub, slp=slp):
                            dv = deltab[:].rearrange("p (c k) -> p c k", k=256)[:, :, 128 * sub:128 * (sub + 1)]
                            av = A3[:, :, n:n + 1].broadcast_to([128, DC, 128])
                            return v.tensor_tensor(dA3_s[slp][:, :, 1:129], dv, av, op=OP.mult)
                        t_dam[u] = mstage("vector", f_dam,
                                          deps=[t_scn[u - 2] if u >= 2 else None, t_fadd, t_ms])

                        t_dae[u] = mstage("scalar", lambda s, slp=slp: s.activation(
                            dA3_s[slp][:, :, 1:129], dA3_s[slp][:, :, 1:129], AF.Exp, scale=1.0),
                            deps=[t_dam[u]])

                        def f_dbu(v, n=n, sub=sub, slp=slp):
                            gv = gb3[:, :, 128 * sub:128 * (sub + 1)]
                            bv_ = PS[6 + n % 2][:, 128 * sub:128 * (sub + 1)]
                            bv_ = bv_.unsqueeze(1).broadcast_to([128, DC, 128])
                            return v.tensor_tensor(dBu3_s[slp][:, :, 1:129], gv, bv_, op=OP.mult)
                        t_dbu = mstage("vector", f_dbu,
                                       deps=[t_bcastk[n % 2], t_g,
                                             t_scn[u - 2] if u >= 2 else None, t_hsv])

                    if 1 <= u:
                        v_ = u - 1
                        nv, subv, slv = v_ // 2, v_ % 2, v_ % 2

                        def f_scan(vv, nv=nv, slv=slv):
                            vv.tensor_copy(dBu3_s[slv][:, :, 0:1], si3[:, :, nv:nv + 1])
                            vv.drain()
                            vv.tensor_tensor_scan(dBu_s[slv], dA_s[slv], dBu_s[slv],
                                                  initial=0.0, op0=OP.mult, op1=OP.add)
                            vv.drain()
                            return vv.tensor_copy(si3[:, :, nv:nv + 1], dBu3_s[slv][:, :, 128:129])
                        t_scn[v_] = mstage("vector", f_scan, deps=[t_dae[v_]])

                        def f_ym(vv, nv=nv, subv=subv, slv=slv):
                            cv = PS[6 + nv % 2][:, 256 + 128 * subv:256 + 128 * (subv + 1)]
                            cv = cv.unsqueeze(1).broadcast_to([128, DC, 128])
                            return vv.tensor_tensor(tmpb3, dBu3_s[slv][:, :, 1:129], cv, op=OP.mult)
                        t_ym[v_] = mstage("vector", f_ym, deps=[])

                        def f_ya(vv, subv=subv):
                            yv = ub3[:, :, 128 * subv:128 * (subv + 1)]
                            return vv.tensor_tensor(yv, yv, tmpb3, op=OP.add)
                        t_ya = mstage("vector", f_ya, deps=[t_ud])
                pn.close()
                t_yadd = {0: t_ya, 1: t_ya}
                t_ymul = {0: t_ym[U - 2], 1: t_ym[U - 1]}

                py_ = contextlib.ExitStack()
                ysp_hi = py_.enter_context(nc.sbuf_tensor(f"ysp_hi_{sc}", [128, DC * 256], BF16))
                ysp_lo = py_.enter_context(nc.sbuf_tensor(f"ysp_lo_{sc}", [128, DC * 256], BF16))
                t_yf = mstage("vector", lambda v: v.tensor_tensor(ub[:], ub[:], sres[:], op=OP.mult),
                              deps=[t_yadd[0], t_yadd[1], t_wevac[OC - 1],
                                    t_ymul[0], t_ymul[1]])

                def f_ys(v):
                    v.tensor_copy(ysp_hi[:], ub[:])
                    v.drain()
                    return v.tensor_tensor(ysp_lo[:], ub[:], ysp_hi[:], op=OP.subtract)
                t_ys = mstage("vector", f_ys, deps=[t_yf])

                t_womm = [None] * DC
                t_wold = [None] * DC
                for d in range(DC):
                    sl = d % 2
                    t_wold[d] = mdmas(
                        [(wob_hi[:, sl * DM:sl * DM + DM], st_woh[128 * d:128 * (d + 1), :]),
                         (wob_lo[:, sl * DM:sl * DM + DM], st_wol[128 * d:128 * (d + 1), :])],
                        sem=f"lwo{sl}", deps=[t_womm[d - 2] if d >= 2 else None])

                    def f_wo(t, d=d, sl=sl):
                        last = None
                        for sub in range(2):
                            yh = ysp_hi[:, 256 * d + 128 * sub:256 * d + 128 * (sub + 1)]
                            yl = ysp_lo[:, 256 * d + 128 * sub:256 * d + 128 * (sub + 1)]
                            for ch in range(2):
                                bank = PS[2 * sub + ch]
                                wh = wob_hi[:, sl * DM + 512 * ch:sl * DM + 512 * (ch + 1)]
                                wl = wob_lo[:, sl * DM + 512 * ch:sl * DM + 512 * (ch + 1)]
                                last = t.matmul(bank[:, 0:512], yh, wh, start=(d == 0), stop=False)
                                last = t.matmul(bank[:, 0:512], yh, wl, start=False, stop=False)
                                last = t.matmul(bank[:, 0:512], yl, wh, start=False, stop=(d == DC - 1))
                        return last
                    deps = [t_ys, t_wold[d]]
                    if d == 0:
                        deps += bfree[0] + bfree[1] + bfree[2] + bfree[3]
                    t_womm[d] = mstage("tensor", f_wo, deps=deps)

                def f_xa(v, i0=i0):
                    last = None
                    for sub in range(2):
                        i = i0 + sub
                        for ch in range(2):
                            bank = PS[2 * sub + ch]
                            xs = X[:, DM * i + 512 * ch:DM * i + 512 * (ch + 1)]
                            last = v.scalar_tensor_tensor(xs, bank[:, 0:512], 0.0, xs,
                                                          op0=OP.add, op1=OP.add)
                    return last
                t_xa = mstage("vector", f_xa, deps=[t_womm[DC - 1]])
                for b in range(4):
                    bfree[b] = [t_xa]
                py_.close()

            if debug:
                dmas([(dbg["dbg_mamba"][128 * i:128 * (i + 1), :], Xt(i)) for i in range(TT)])

        emit_ffn("f2")

        dmas([(out[128 * i:128 * (i + 1), :], Xt(i)) for i in range(TT)])
        stage("sync", lambda e: None, inc=0, auto=False)

    return nc


def kernel(**inputs):
    T = 2048
    x = np.ascontiguousarray(inputs["L0_attn_a_x"].astype(np.float32, copy=False))
    w = _prep_weights(inputs)
    nc = build(T=T, debug=False)
    in_maps = []
    for c in range(NCORE):
        m = dict(w)
        m["x"] = np.ascontiguousarray(x[c])
        in_maps.append(m)
    res = run_bass_kernel_spmd(nc, in_maps, core_ids=list(range(NCORE)))
    return np.stack([np.asarray(res.results[c]["out"]) for c in range(NCORE)], axis=0)
